# version 20
# baseline (speedup 1.0000x reference)
import numpy as np
from contextlib import ExitStack

import concourse.bass as bass
import concourse.mybir as mybir
from concourse.bass import IndirectOffsetOnAxis
from concourse.bass_utils import run_bass_kernel_spmd

F32 = mybir.dt.float32
AF = mybir.ActivationFunctionType
ALU = mybir.AluOpType
AX = mybir.AxisListType

D = 1024
DC = 8
NH = 16
DH = 64
NPAIR = 8
S = 4096
B = 4
TQ = 2176
NT_Q = 17
TKV = 4224
NT_KV = 33
QOFF = 2048
QCS = [(0, 512), (512, 512), (1024, 512), (1536, 512), (2048, 128)]
E = 8
HD = 2048
NHC = 16
NSLOT = 34
NS2 = NSLOT * 2
RTROWS = NSLOT * 256
YROWS = 5 * TQ
WIN = 128
ENGS = ("tensor", "vector", "scalar", "sync", "gpsimd")
DEBUG_STREAMS = False
STOP_AFTER_FFN1 = False
_LAST_PROG = None


def attn_a_live():
    out = []
    for qi, (qs, qw) in enumerate(QCS):
        kmax = min(NT_KV - 1, (QOFF + qs + qw - 1) // 128)
        for kt in range(kmax + 1):
            out.append((qi, kt))
    return out


def attn_a_deltas(parity):
    ds = []
    for (qi, kt) in attn_a_live():
        qs, qw = QCS[qi]
        d = 128 * kt - qs - QOFF
        if parity == 0 and kt < 16:
            d = 10000.0
        if parity == 1 and kt == 32:
            d = 10000.0
        ds.append(float(d))
    return np.array(ds, dtype=np.float32)


def attn_b_live():
    out = []
    for qi, (qs, qw) in enumerate(QCS):
        k0 = max(0, -(-(qs - 127) // 128))
        k1 = min(NT_Q - 1, (qs + qw + 126) // 128)
        for k in range(k0, k1 + 1):
            out.append((qi, k))
    return out


def attn_b_bmask(parity):
    tiles = []
    for (qi, k) in attn_b_live():
        qs, qw = QCS[qi]
        o = 128 * k - qs
        ki = np.arange(128)[:, None]
        q_in = np.arange(512)[None, :]
        rel = o + ki - q_in
        t = np.where((rel >= 0) & (rel < WIN), -rel.astype(np.float64), -1e32)
        if parity == 1 and k == 16 and qi != 4:
            t = np.full_like(t, -1e32)
        tiles.append(t.astype(np.float32))
    return np.stack(tiles)


LIVE_A = attn_a_live()
LIVE_B = attn_b_live()
N_LIVE_A = len(LIVE_A)
N_LIVE_B = len(LIVE_B)


class Buf:
    __slots__ = ("name", "writer", "readers")

    def __init__(self, name):
        self.name = name
        self.writer = None
        self.readers = []


N_DMA_SEMS = 72
N_HW_SEMS = 36


class Op:
    __slots__ = ("eng", "fn", "dma", "tag", "eidx", "dsem", "dval",
                 "need_inc", "waits", "iname", "rw")

    def __init__(self, eng, fn, dma, tag):
        self.eng = eng
        self.fn = fn
        self.dma = dma
        self.tag = tag
        self.eidx = -1
        self.dsem = -1
        self.dval = 0
        self.need_inc = False
        self.waits = []


class Prog:
    def __init__(self, nc):
        self.nc = nc
        self.ops = {e: [] for e in ENGS}
        self.dcount = [0] * N_DMA_SEMS
        self.bufsem = {}
        self.sem_next = 0
        self.sw_bufsem = {}
        self.sw_next = N_HW_SEMS

    def _sem_for(self, buf, eng):
        k = id(buf)
        if eng == "gpsimd":
            s = self.sw_bufsem.get(k)
            if s is None:
                if self.sw_next >= N_DMA_SEMS:
                    raise RuntimeError("swdge dma sem pool exhausted")
                s = self.sw_next
                self.sw_next += 1
                self.sw_bufsem[k] = s
            return s
        s = self.bufsem.get(k)
        if s is None:
            if self.sem_next >= N_HW_SEMS:
                raise RuntimeError("hwdge dma sem pool exhausted in one epoch")
            s = self.sem_next
            self.sem_next += 1
            self.bufsem[k] = s
        return s

    def emit(self, eng, fn, reads=(), writes=(), dma=False, tag=""):
        op = Op(eng, fn, dma, tag)
        op.rw = ([b.name for b in reads], [b.name for b in writes])
        op.eidx = len(self.ops[eng])
        self.ops[eng].append(op)
        deps = []
        for b in reads:
            if b.writer is not None:
                deps.append(b.writer)
        for b in writes:
            if b.writer is not None:
                deps.append(b.writer)
            deps.extend(b.readers)
        seen = set()
        for p in deps:
            if id(p) in seen or p is op:
                continue
            seen.add(id(p))
            if (p.eng == op.eng == "tensor") and not p.dma and not op.dma:
                continue
            if p.dma:
                op.waits.append(("dma", p.dsem, 16 * self.dcount[p.dsem]))
            else:
                op.waits.append(p)
                p.need_inc = True
        if dma:
            assert writes, "dma op must declare its destination buf first"
            s = self._sem_for(writes[0], eng)
            self.dcount[s] += 1
            op.dsem = s
            op.dval = 16 * self.dcount[s]
        for b in reads:
            b.readers.append(op)
        for b in writes:
            b.writer = op
            b.readers = []
        return op

    def barrier(self, bufs_to_reset=()):
        snapshot = {e: (self.ops[e][-1] if self.ops[e] else None) for e in ENGS}
        for p in snapshot.values():
            if p is not None and not p.dma:
                p.need_inc = True
        dwaits = [("dma", s, 16 * self.dcount[s])
                  for s in range(N_DMA_SEMS) if self.dcount[s]]
        for e in ENGS:
            op = Op(e, lambda eng: eng.nop(), False, "barrier")
            op.eidx = len(self.ops[e])
            for f, p in snapshot.items():
                if p is None:
                    continue
                if p.dma:
                    continue
                if f == e:
                    continue
                op.waits.append(p)
            op.waits.extend(dwaits)
            self.ops[e].append(op)
        self.bufsem = {}
        self.sem_next = 0
        self.sw_bufsem = {}
        self.sw_next = N_HW_SEMS
        for b in bufs_to_reset:
            b.writer = None
            b.readers = []

    def materialize(self, stack):
        nc = self.nc
        psem = {e: stack.enter_context(nc.semaphore(f"pg_{e}")) for e in ENGS}
        dsems = [stack.enter_context(nc.semaphore(f"dm{i}"))
                 for i in range(N_DMA_SEMS)]
        pcount = {}
        for e in ENGS:
            c = 0
            arr = []
            for op in self.ops[e]:
                if op.need_inc and not op.dma:
                    c += 1
                arr.append(c)
            pcount[e] = arr

        def run(e, eng):
            hi = {}
            for op in self.ops[e]:
                for p in op.waits:
                    if isinstance(p, tuple):
                        _, s, val = p
                        key = ("d", s)
                        sem = dsems[s]
                    else:
                        val = pcount[p.eng][p.eidx]
                        key = ("p", p.eng)
                        sem = psem[p.eng]
                    if hi.get(key, -1) >= val:
                        continue
                    hi[key] = val
                    eng.wait_ge(sem, val)
                inst = op.fn(eng)
                op.iname = getattr(inst, 'name', None)
                if op.dma:
                    inst.then_inc(dsems[op.dsem], 16)
                elif op.need_inc:
                    inst.then_inc(psem[e], 1)
            if e in ("gpsimd", "sync"):
                for s in range(N_DMA_SEMS):
                    if self.dcount[s]:
                        eng.wait_ge(dsems[s], 16 * self.dcount[s])

        with nc.Block() as blk:
            blk.tensor(lambda eng: run("tensor", eng))
            blk.vector(lambda eng: run("vector", eng))
            blk.scalar(lambda eng: run("scalar", eng))
            blk.sync(lambda eng: run("sync", eng))
            blk.gpsimd(lambda eng: run("gpsimd", eng))


class TRef:
    __slots__ = ("ap", "buf")

    def __init__(self, ap, buf):
        self.ap = ap
        self.buf = buf

    def __getitem__(self, idx):
        return TRef(self.ap[idx], self.buf)


class Arena:
    def __init__(self, nc, name, words, dtype=F32):
        self.h = nc.alloc_sbuf_tensor(name, [128, words], dtype)
        self.ap = self.h.ap()
        self.off = 0
        self.words = words

    def alloc(self, name, w, parts=128):
        w16 = (w + 7) & ~7
        assert self.off + w16 <= self.words, (
            f"SBUF arena overflow at {name}: {self.off}+{w16} > {self.words}")
        ap = self.ap[0:parts, self.off:self.off + w]
        self.off += w16
        return TRef(ap, Buf(name))

    def mark(self):
        return self.off

    def reset(self, off):
        self.off = off


def build_nc():
    nc = bass.Bass("TRN2", target_bir_lowering=False)
    f = nc.dram_tensor

    xkv_d = f("xkv", [TKV, D], F32, kind="ExternalInput").ap()
    wq_d = f("wq", [NPAIR, 128, 1024], F32, kind="ExternalInput").ap()
    wk_d = f("wk", [NPAIR, 128, 1024], F32, kind="ExternalInput").ap()
    wv_d = f("wv", [NPAIR, 128, 1024], F32, kind="ExternalInput").ap()
    wo_d = f("wo", [DC, 128, 1024], F32, kind="ExternalInput").ap()
    bwq_d = f("bwq", [NPAIR, 128, 1024], F32, kind="ExternalInput").ap()
    bwk_d = f("bwk", [NPAIR, 128, 1024], F32, kind="ExternalInput").ap()
    bwv_d = f("bwv", [NPAIR, 128, 1024], F32, kind="ExternalInput").ap()
    bwo_d = f("bwo", [DC, 128, 1024], F32, kind="ExternalInput").ap()
    rw1_d = f("rw1", [DC, 128, 1024], F32, kind="ExternalInput").ap()
    rw2_d = f("rw2", [128, 64], F32, kind="ExternalInput").ap()
    rb1_d = f("rb1", [128, 8], F32, kind="ExternalInput").ap()
    rb2_d = f("rb2", [1, 8], F32, kind="ExternalInput").ap()
    eb1_d = f("eb1", [128, 128], F32, kind="ExternalInput").ap()
    eb1g_d = f("eb1g", [E, 128, 16], F32, kind="ExternalInput").ap()
    eb2_d = f("eb2", [8, 1024], F32, kind="ExternalInput").ap()
    ew1_d = f("ew1", [E, 128, 16384], F32, kind="ExternalInput").ap()
    ew2_d = f("ew2", [E, NHC, 128, 1024], F32, kind="ExternalInput").ap()
    deltas_d = f("deltas", [128, N_LIVE_A], F32, kind="ExternalInput").ap()
    bmask_d = f("bmask", [N_LIVE_B, 128, 512], F32, kind="ExternalInput").ap()

    yout_d = f("y_out", [TQ, D], F32, kind="ExternalOutput").ap()

    _sk = "ExternalOutput" if DEBUG_STREAMS else "Internal"
    x2_d = f("x2s", [NT_Q, 128, D], F32, kind=_sk).ap()
    x3_d = f("x3s", [NT_Q, 128, D], F32, kind=_sk).ap()
    x4_d = f("x4s", [NT_Q, 128, D], F32, kind=_sk).ap()
    ao_d = f("aos", [NPAIR, 128, TQ], F32, kind=_sk).ap()
    BF16_ = mybir.dt.bfloat16
    qbh_d = f("qbhs", [NPAIR, 128, TQ], BF16_, kind="Internal").ap()
    qbl_d = f("qbls", [NPAIR, 128, TQ], BF16_, kind="Internal").ap()
    kbh_d = f("kbhs", [NPAIR, 128, TQ], BF16_, kind="Internal").ap()
    kbl_d = f("kbls", [NPAIR, 128, TQ], BF16_, kind="Internal").ap()
    vb_d = f("vbs", [NPAIR, NT_Q, 128, 130], F32, kind="Internal").ap()
    xhsp_d = f("xhsp", [NT_KV, 2, 128, 1024], BF16_, kind="Internal").ap()
    ew1sp_d = f("ew1sp", [NHC, E, 128, 2, 1024], BF16_, kind="Internal").ap()
    ew2sp_d = f("ew2sp", [NHC, E, 128, 2, 1024], BF16_, kind="Internal").ap()
    rw1sp_d = f("rw1sp", [DC, 2, 128, 1024], BF16_, kind="Internal").ap()
    wosp_d = f("wosp", [DC, 2, 128, 1024], BF16_, kind="Internal").ap()
    wobsp_d = f("wobsp", [DC, 2, 128, 1024], BF16_, kind="Internal").ap()
    xhtm_d = f("xhtm", [NT_Q, 128, 1024], F32, kind="Internal").ap()
    rt_d = f("rt", [RTROWS + 1, 8], F32, kind="Internal").ap()
    ysl_d = f("ysl", [YROWS, 1024], F32, kind="Internal").ap()

    dbg_d = None
    dbg_ctr = [0]
    if DEBUG_STREAMS:
        dbg_d = f("dbg", [64, 128, 512], F32, kind="ExternalOutput").ap()

    ident_np = np.eye(128, dtype=np.float32)
    mstrict_np = (np.arange(128)[:, None] < np.arange(128)[None, :]
                  ).astype(np.float32)
    u8_np = np.zeros((8, 8), np.float32)
    u8_np[np.triu_indices(8, 1)] = 1.0
    iota34_np = np.arange(NSLOT, dtype=np.float32).reshape(1, NSLOT)
    tokf_np = (np.arange(NT_Q)[None, :] * 128
               + np.arange(128)[:, None]).astype(np.float32)
    rtfill_np = np.zeros((128, NS2 * 8), np.float32)
    for s in range(NS2):
        rtfill_np[:, s * 8 + 2] = 4 * TQ + s
    dqk_np = (np.arange(512)[None, :] - np.arange(128)[:, None]).astype(np.float32)
    hvals_np = np.repeat(np.arange(16, dtype=np.float32)[None, :], 128, axis=0)
    ident_d = nc.inline_tensor(ident_np, name="identc").ap()
    dqk_d = nc.inline_tensor(dqk_np, name="dqkc").ap()
    hvals_d = nc.inline_tensor(np.ascontiguousarray(hvals_np), name="hvalsc").ap()
    mstrict_d = nc.inline_tensor(mstrict_np, name="mstrictc").ap()
    u8_d = nc.inline_tensor(u8_np, name="u8c").ap()
    iota34_d = nc.inline_tensor(iota34_np, name="iota34c").ap()
    tokf_d = nc.inline_tensor(tokf_np, name="tokfc").ap()
    rtfill_d = nc.inline_tensor(rtfill_np, name="rtfillc").ap()

    P = Prog(nc)
    global _LAST_PROG
    _LAST_PROG = P
    stack = ExitStack()

    pb = []
    for i in range(8):
        h = nc.alloc_psum_tensor(f"pb{i}", [128, 512], F32)
        pb.append(TRef(h.ap(), Buf(f"pb{i}")))

    BF16 = mybir.dt.bfloat16
    ar = Arena(nc, "arena", 28800)
    ar16 = Arena(nc, "arena16", 48400, BF16)

    ident = ar.alloc("ident", 128)
    ones = ar.alloc("ones", 128)
    hvals = ar.alloc("hvals", 16)
    slope = ar.alloc("slope", 16)
    slt1 = ar.alloc("slt1", 16)
    slt2 = ar.alloc("slt2", 16)
    constv = ar.alloc("constv", 8)
    _CONSTS = [1e-5, -1.0, 0.125]
    for i, v in enumerate(_CONSTS):
        nc.const_aps.register(F32, v, constv.ap[:, i:i + 1])
    persist_off = ar.mark()

    dbuf = {n: Buf(n) for n in ("xkv", "x2", "x3", "x4", "ao", "vb", "yout",
                                "qbh", "qbl", "kbh", "kbl", "wsp", "xhsp",
                                "xhtm", "rt", "ysl0", "ysl1", "ysl2", "ysl3",
                                "win")}

    def dma(dst, src, eng="sync", rb=None, wb=None, tag="dma"):
        reads = [src.buf] + (rb or [])
        writes = [dst.buf] + (wb or [])
        P.emit(eng, (lambda e, o=dst.ap, i=src.ap: e.dma_start(out=o, in_=i)),
               reads, writes, dma=True, tag=tag)

    def mm(out, lhsT, rhs, start, stop, tag="mm"):
        P.emit("tensor",
               (lambda e, o=out.ap, a=lhsT.ap, b=rhs.ap, s=start, t=stop:
                e.matmul(o, a, b, start=s, stop=t)),
               [lhsT.buf, rhs.buf], [out.buf], tag=tag)

    def trans(out, in_, tag="tr"):
        P.emit("tensor",
               (lambda e, o=out.ap, i=in_.ap, idn=ident.ap: e.transpose(o, i, idn)),
               [in_.buf, ident.buf], [out.buf], tag=tag)

    def act(out, in_, func, bias=0.0, scale=1.0, accum=None, eng="scalar", tag="act"):
        b_ap = bias.ap if isinstance(bias, TRef) else bias
        s_ap = scale.ap if isinstance(scale, TRef) else scale
        reads = [in_.buf]
        if isinstance(bias, TRef):
            reads.append(bias.buf)
        if isinstance(scale, TRef):
            reads.append(scale.buf)
        writes = [out.buf]
        if accum is not None:
            writes.append(accum.buf)
        a_ap = accum.ap if accum is not None else None
        P.emit(eng,
               (lambda e, o=out.ap, i=in_.ap, fn=func, bb=b_ap, ss=s_ap, aa=a_ap:
                e.activation(o, i, fn, bias=bb, scale=ss, accum_out=aa)),
               reads, writes, tag=tag)

    def ts(out, in0, s1, op0, s2=None, op1=None, eng="vector", tag="ts"):
        s1a = s1.ap if isinstance(s1, TRef) else s1
        s2a = s2.ap if isinstance(s2, TRef) else s2
        reads = [in0.buf]
        if isinstance(s1, TRef):
            reads.append(s1.buf)
        if isinstance(s2, TRef):
            reads.append(s2.buf)
        P.emit(eng,
               (lambda e, o=out.ap, i=in0.ap, a=s1a, b=s2a, o0=op0, o1=op1:
                e.tensor_scalar(o, i, a, b, o0) if o1 is None
                else e.tensor_scalar(o, i, a, b, o0, o1)),
               reads, [out.buf], tag=tag)

    def stt(out, in0, s, in1, op0, op1, eng="vector", tag="stt"):
        sa = s.ap if isinstance(s, TRef) else s
        reads = [in0.buf, in1.buf]
        if isinstance(s, TRef):
            reads.append(s.buf)
        P.emit(eng,
               (lambda e, o=out.ap, i0=in0.ap, ss=sa, i1=in1.ap, a=op0, b=op1:
                e.scalar_tensor_tensor(o, i0, ss, i1, a, b)),
               reads, [out.buf], tag=tag)

    def tt(out, a, b, op, eng="vector", tag="tt"):
        P.emit(eng,
               (lambda e, o=out.ap, x=a.ap, y=b.ap, oo=op: e.tensor_tensor(o, x, y, oo)),
               [a.buf, b.buf], [out.buf], tag=tag)

    def red(out, in_, op, negate=False, eng="vector", tag="red"):
        P.emit(eng,
               (lambda e, o=out.ap, i=in_.ap, oo=op, n=negate:
                e.tensor_reduce(o, i, AX.X, oo, negate=n)),
               [in_.buf], [out.buf], tag=tag)

    def recip(out, in_, eng="vector", tag="rcp"):
        P.emit(eng, (lambda e, o=out.ap, i=in_.ap: e.reciprocal(o, i)),
               [in_.buf], [out.buf], tag=tag)

    def copy(out, in_, eng="vector", scale=None, tag="cp"):
        if scale is not None:
            act(out, in_, AF.Copy, scale=scale, eng="scalar", tag=tag)
        elif eng == "scalar":
            P.emit(eng, (lambda e, o=out.ap, i=in_.ap: e.copy(o, i)),
                   [in_.buf], [out.buf], tag=tag)
        else:
            P.emit(eng, (lambda e, o=out.ap, i=in_.ap: e.tensor_copy(o, i)),
                   [in_.buf], [out.buf], tag=tag)

    def memset(dst, val, eng="vector", tag="ms"):
        P.emit(eng, (lambda e, o=dst.ap, v=val: e.memset(o, v)),
               [], [dst.buf], tag=tag)

    def dref(ap, name):
        return TRef(ap, dbuf[name])

    dbg_buf = Buf("dbg")

    def dbg_dump(src, parts=128, w=512):
        if dbg_d is None or dbg_ctr[0] >= 64:
            return
        i = dbg_ctr[0]
        dbg_ctr[0] += 1
        P.emit("gpsimd",
               (lambda e, o=dbg_d[i][0:parts, 0:w], s=src.ap:
                e.dma_start(out=o, in_=s)),
               [src.buf], [dbg_buf], dma=True, tag="dbg")

    def mm3(out, lh, rh, start, stop, tag="mm3"):
        combos = [(lh[0], rh[0]), (lh[0], rh[1]), (lh[1], rh[0])]
        for i, (a, b_) in enumerate(combos):
            mm(out, a, b_, start=(start and i == 0), stop=(stop and i == 2),
               tag=tag)

    def split(hi, lo, src, scale=None):
        if scale is None:
            copy(hi, src, eng="scalar")
            tt(lo, src, hi, ALU.subtract)
        else:
            act(hi, src, AF.Copy, scale=scale)
            stt(lo, src, scale, hi, ALU.mult, ALU.subtract)

    prep_f32 = [ar.alloc(f"prep{i}", 1024) for i in range(2)]
    prep_h = [ar16.alloc(f"preph{i}", 1024) for i in range(2)]
    prep_l = [ar16.alloc(f"prepl{i}", 1024) for i in range(2)]
    prep_units = []
    for c in range(DC):
        prep_units.append((wo_d[c], wosp_d[c][0], wosp_d[c][1]))
    for c in range(DC):
        prep_units.append((rw1_d[c], rw1sp_d[c][0], rw1sp_d[c][1]))
    for e in range(E):
        for ch in range(16):
            prep_units.append((ew1_d[e][:, ch * 1024:(ch + 1) * 1024],
                               ew1sp_d[ch, e][:, 0, :],
                               ew1sp_d[ch, e][:, 1, :]))
    for e in range(E):
        for hc in range(NHC):
            prep_units.append((ew2_d[e, hc], ew2sp_d[hc, e][:, 0, :],
                               ew2sp_d[hc, e][:, 1, :]))
    for c in range(DC):
        prep_units.append((bwo_d[c], wobsp_d[c][0], wobsp_d[c][1]))
    prep_pos = [0]

    def drain_prep(n):
        i0 = prep_pos[0]
        for i in range(i0, min(i0 + n, len(prep_units))):
            src_ap, dst_hi, dst_lo = prep_units[i]
            pin = prep_f32[i % 2]
            ph = prep_h[i % 2]
            pl = prep_l[i % 2]
            dma(pin, TRef(src_ap, Buf("prepsrc")), eng="gpsimd")
            split(ph, pl, pin)
            dma(TRef(dst_hi, dbuf["wsp"]), ph, eng="gpsimd")
            dma(TRef(dst_lo, dbuf["wsp"]), pl, eng="gpsimd")
        prep_pos[0] = min(i0 + n, len(prep_units))

    dma(ident, TRef(ident_d, Buf("identd")))
    dma(hvals, TRef(hvals_d, Buf("hvalsd")))
    memset(TRef(ones.ap[0:1, :], ones.buf), 1.0)

    a15 = float(np.float32(1.0 / 15.0))
    b15 = float(np.float32(1.0 / 15.0 - a15))
    ln2 = float(np.log(2.0))
    ln2hi = float(np.float32(ln2))
    ln2lo = float(np.float32(ln2 - ln2hi))
    ts(slt1, hvals, a15, ALU.mult)
    ts(slt2, hvals, b15, ALU.mult)
    tt(slt1, slt1, slt2, ALU.add)
    ts(slt1, slt1, -8.0, ALU.mult)
    ts(slt2, slt1, ln2lo, ALU.mult)
    ts(slt1, slt1, ln2hi, ALU.mult)
    tt(slt1, slt1, slt2, ALU.add)
    act(slope, slt1, AF.Exp)
    for i, v in enumerate(_CONSTS):
        memset(TRef(constv.ap[:, i:i + 1], constv.buf), v)
    P.barrier()

    def emit_ln(xin, xh, stats, scratch):
        ss = TRef(stats.ap[:, 0:1], stats.buf)
        sm = TRef(stats.ap[:, 1:2], stats.buf)
        mean = TRef(stats.ap[:, 2:3], stats.buf)
        ssn = TRef(stats.ap[:, 3:4], stats.buf)
        nv = TRef(stats.ap[:, 4:5], stats.buf)
        sd = TRef(stats.ap[:, 5:6], stats.buf)
        rstd = TRef(stats.ap[:, 6:7], stats.buf)
        act(scratch, xin, AF.Square, accum=ss)
        red(sm, xin, ALU.add)
        ts(mean, sm, 1.0 / D, ALU.mult)
        ts(ssn, ss, 1.0 / D, ALU.mult)
        ts(nv, mean, mean, ALU.mult, ssn, ALU.subtract)
        act(sd, nv, AF.Sqrt, bias=1e-5, scale=-1.0)
        recip(rstd, sd)
        ts(xh, xin, mean, ALU.subtract, rstd, ALU.mult)

    ov = ar.mark()
    ov16 = ar16.mark()
    dqk = ar.alloc("dqk", 512)
    deltas = ar.alloc("deltas", N_LIVE_A)
    dma(dqk, TRef(dqk_d, Buf("dqkd")))
    dma(deltas, TRef(deltas_d, Buf("deltasd")))
    vtm = ar.alloc("vtm", NT_KV * 130)
    ov1 = ar.mark()
    ov16_1 = ar16.mark()
    xin2 = [ar.alloc(f"xin{i}", 1024) for i in range(2)]
    xh2 = [ar.alloc(f"xh{i}", 1024) for i in range(2)]
    stat2 = [ar.alloc(f"st{i}", 8) for i in range(2)]
    scr2 = [ar.alloc(f"scr{i}", 1024) for i in range(2)]
    stg = [[ar16.alloc(f"stg{hl}{i}", 1024) for i in range(2)] for hl in range(2)]

    for t in range(NT_KV):
        xi = xin2[t % 2]
        xh = xh2[t % 2]
        dma(xi, dref(xkv_d[t * 128:(t + 1) * 128, :], "xkv"))
        emit_ln(xi, xh, stat2[t % 2], scr2[t % 2])
        for half in range(2):
            p = pb[(2 * t + half) % 4]
            for cc in range(4):
                c = half * 4 + cc
                trans(TRef(p.ap[:, cc * 128:(cc + 1) * 128], p.buf),
                      TRef(xh.ap[:, c * 128:(c + 1) * 128], xh.buf))
            split(TRef(stg[0][t % 2].ap[:, half * 512:(half + 1) * 512],
                       stg[0][t % 2].buf),
                  TRef(stg[1][t % 2].ap[:, half * 512:(half + 1) * 512],
                       stg[1][t % 2].buf),
                  TRef(p.ap[:, 0:512], p.buf))
        for hl in range(2):
            dma(TRef(xhsp_d[t, hl], dbuf["xhsp"]),
                TRef(stg[hl][t % 2].ap, stg[hl][t % 2].buf), eng="gpsimd")
        drain_prep(5)

    P.barrier()
    ar.reset(ov1)
    ar16.reset(ov16_1)
    wsl = [ar.alloc("wsl0", 3 * 1024)]
    es = [[ar.alloc(f"es{h}{i}", 512) for i in range(3)] for h in range(2)]
    rd_sb = [ar.alloc(f"rd{h}", 512) for h in range(2)]
    ast = [ar.alloc(f"ast{i}", 512) for i in range(2)]
    kt2h = ar16.alloc("kt2h", TKV)
    kt2l = ar16.alloc("kt2l", TKV)
    qt2h = ar16.alloc("qt2h", TQ)
    qt2l = ar16.alloc("qt2l", TQ)
    w16 = ar16.alloc("w16", 6 * 1024)
    xstr = [ar16.alloc(f"xstr{i}", DC * 2 * 512) for i in range(3)]
    KV_CHUNKS = [(i * 512, 512) for i in range(8)] + [(4096, 128)]

    def _wslice(base, c):
        return (TRef(w16.ap[:, base * 1024 + c * 128:base * 1024 + (c + 1) * 128],
                     w16.buf),
                TRef(w16.ap[:, (base + 3) * 1024 + c * 128:
                            (base + 3) * 1024 + (c + 1) * 128], w16.buf))

    li_of = {}
    for li, (qi, kt) in enumerate(LIVE_A):
        li_of[(qi, kt)] = li

    def emit_chunk_loads(ci):
        cs_, cw_ = KV_CHUNKS[ci]
        xs_ = xstr[ci % 3]
        for ti in range(cw_ // 128):
            for hl in range(2):
                dma(TRef(xs_.ap[:, (ti * 2 + hl) * 1024:
                                (ti * 2 + hl + 1) * 1024], xs_.buf),
                    TRef(xhsp_d[cs_ // 128 + ti, hl], dbuf["xhsp"]))

    def emit_wload_loads(pr):
        w = wsl[0]
        dma(TRef(w.ap[:, 0:1024], w.buf), TRef(wq_d[pr], Buf("wqd")))
        dma(TRef(w.ap[:, 1024:2048], w.buf), TRef(wk_d[pr], Buf("wkd")))
        dma(TRef(w.ap[:, 2048:3072], w.buf), TRef(wv_d[pr], Buf("wvd")))

    def emit_wsplit():
        w = wsl[0]
        for b in range(3):
            wp = TRef(w.ap[:, b * 1024:(b + 1) * 1024], w.buf)
            hi = TRef(w16.ap[:, b * 1024:(b + 1) * 1024], w16.buf)
            lo = TRef(w16.ap[:, (b + 3) * 1024:(b + 4) * 1024], w16.buf)
            copy(hi, wp, eng="gpsimd")
            tt(lo, wp, hi, ALU.subtract, eng="gpsimd")

    def emit_wload(pr):
        emit_wload_loads(pr)
        emit_wsplit()

    for pr in range(NPAIR):
        if pr == 0:
            emit_wload(0)
            emit_chunk_loads(0)
            emit_chunk_loads(1)
        memset(vtm, 1.0)
        for ci, (cs, cw) in enumerate(KV_CHUNKS):
            if ci + 2 < len(KV_CHUNKS):
                emit_chunk_loads(ci + 2)
            elif pr + 1 < NPAIR:
                emit_chunk_loads(ci + 2 - len(KV_CHUNKS))
            if ci == 6 and pr + 1 < NPAIR:
                emit_wload_loads(pr + 1)
            xs = xstr[ci % 3]
            ntile = cw // 128
            xsv = xs.ap[:, 0:ntile * 2048].rearrange(
                "p (ti hl cj) -> p ti hl cj", hl=2, cj=1024)

            def _x(c, hl, j0, jw, _xsv=xsv, _xs=xs):
                if jw == 128 and j0 % 128 == 0:
                    ti = j0 // 128
                    return TRef(_xsv[:, ti, hl, c * 128:(c + 1) * 128], _xs.buf)
                assert j0 == 0
                return TRef(_xsv[:, :, hl, c * 128:(c + 1) * 128], _xs.buf)

            po = TRef(pb[4 + ci % 2].ap[:, 0:cw], pb[4 + ci % 2].buf)
            for c in range(DC):
                mm3(po, _wslice(1, c), (_x(c, 0, 0, cw), _x(c, 1, 0, cw)),
                    start=(c == 0), stop=(c == DC - 1), tag="kproj")
            split(TRef(kt2h.ap[:, cs:cs + cw], kt2h.buf),
                  TRef(kt2l.ap[:, cs:cs + cw], kt2l.buf), po)
            for ti in range(cw // 128):
                t = cs // 128 + ti
                po2 = TRef(pb[6 + t % 2].ap[:, 0:128], pb[6 + t % 2].buf)
                for c in range(DC):
                    xh_ = _x(c, 0, ti * 128, 128)
                    xl_ = _x(c, 1, ti * 128, 128)
                    wh_, wl_ = _wslice(2, c)
                    for i, (a, b_) in enumerate([(xh_, wh_), (xh_, wl_),
                                                 (xl_, wh_)]):
                        mm(po2, a, b_, start=(c == 0 and i == 0),
                           stop=(c == DC - 1 and i == 2), tag="vproj")
                copy(TRef(vtm.ap[:, t * 130:t * 130 + 64], vtm.buf),
                     TRef(pb[6 + t % 2].ap[:, 0:64], pb[6 + t % 2].buf))
                copy(TRef(vtm.ap[:, t * 130 + 65:t * 130 + 129], vtm.buf),
                     TRef(pb[6 + t % 2].ap[:, 64:128], pb[6 + t % 2].buf),
                     eng="scalar")
            if ci <= 5:
                drain_prep(2)
            if ci >= 4:
                qi = ci - 4
                qs, qw = QCS[qi]
                po3 = TRef(pb[2 + ci % 2].ap[:, 0:qw], pb[2 + ci % 2].buf)
                for c in range(DC):
                    mm3(po3, _wslice(0, c), (_x(c, 0, 0, qw), _x(c, 1, 0, qw)),
                        start=(c == 0), stop=(c == DC - 1), tag="qproj")
                split(TRef(qt2h.ap[:, qs:qs + qw], qt2h.buf),
                      TRef(qt2l.ap[:, qs:qs + qw], qt2l.buf), po3, scale=0.125)
        if pr + 1 < NPAIR:
            emit_wsplit()
        for qi, (qs, qw) in enumerate(QCS):
            kts = [kt for (qj, kt) in LIVE_A if qj == qi]
            nk = len(kts)
            pvp = [TRef(pb[6].ap[0:65, 0:qw], pb[6].buf),
                   TRef(pb[7].ap[0:65, 0:qw], pb[7].buf)]

            def emit_pv(ki, kt, _pvp=pvp, _nk=nk, _qw=qw):
                for h in range(2):
                    e_t = TRef(es[h][ki % 3].ap[:, 0:_qw], es[h][ki % 3].buf)
                    mm(_pvp[h], TRef(vtm.ap[:, kt * 130 + h * 65:
                                            kt * 130 + (h + 1) * 65], vtm.buf),
                       e_t, start=(ki == 0), stop=(ki == _nk - 1), tag="pv")

            pend = []
            for ki, kt in enumerate(kts):
                li = li_of[(qi, kt)]
                dl = TRef(deltas.ap[:, li:li + 1], deltas.buf)
                d_univ = 128 * kt - qs - QOFF
                openall = (d_univ <= -127) and (16 <= kt <= 31)
                pos = [TRef(pb[3 * h + ki % 3].ap[:, 0:qw],
                            pb[3 * h + ki % 3].buf) for h in range(2)]
                qk_ops = [(kt2h, qt2h), (kt2h, qt2l), (kt2l, qt2h)]
                for pi, (ka, qb_) in enumerate(qk_ops):
                    for h in range(2):
                        mm(pos[h],
                           TRef(ka.ap[h * 64:(h + 1) * 64,
                                      kt * 128:(kt + 1) * 128], ka.buf),
                           TRef(qb_.ap[h * 64:(h + 1) * 64, qs:qs + qw], qb_.buf),
                           start=(pi == 0), stop=(pi == 2), tag="qk")
                if len(pend) >= 2:
                    emit_pv(*pend.pop(0))
                for h in range(2):
                    e_t = TRef(es[h][ki % 3].ap[:, 0:qw], es[h][ki % 3].buf)
                    act(e_t, pos[h], AF.Exp, tag="exp")
                    if not openall:
                        stt(e_t, TRef(dqk.ap[:, 0:qw], dqk.buf), dl, e_t,
                            ALU.is_ge, ALU.mult, tag="zmask")
                pend.append((ki, kt))
            for p_ in pend:
                emit_pv(*p_)
            for h in range(2):
                rds = TRef(rd_sb[h].ap[0:1, 0:qw], rd_sb[h].buf)
                recip(rds, TRef(pvp[h].ap[64:65, :], pvp[h].buf))
                pbc = pb[h]
                bc = TRef(pbc.ap[0:64, 0:qw], pbc.buf)
                mm(bc, TRef(ones.ap[0:1, 0:64], ones.buf), rds,
                   start=True, stop=True, tag="bcast")
                bcs = TRef(es[h][0].ap[0:64, 0:qw], es[h][0].buf)
                copy(bcs, bc, eng="scalar")
                a_s = ast[qi % 2]
                tt(TRef(a_s.ap[h * 64:(h + 1) * 64, 0:qw], a_s.buf),
                   TRef(pvp[h].ap[0:64, :], pvp[h].buf), bcs, ALU.mult)
            a_s = ast[qi % 2]
            dma(dref(ao_d[pr][:, qs:qs + qw], "ao"),
                TRef(a_s.ap[:, 0:qw], a_s.buf), eng="gpsimd")


    P.barrier()
    ar.reset(ov)
    ar16.reset(ov16)
    drain_prep(40)
    wo16 = ar16.alloc("wo16", DC * 2048)
    aoin = [ar.alloc(f"aoin{i}", 1024) for i in range(2)]
    aoi16 = [ar16.alloc(f"aoi16{i}", 2048) for i in range(2)]
    xres = [ar.alloc(f"xres{i}", 1024) for i in range(2)]
    x2st = [ar.alloc(f"x2st{i}", 1024) for i in range(2)]
    for c in range(DC):
        dma(TRef(wo16.ap[:, c * 2048:(c + 1) * 2048].rearrange(
            "p (hl j) -> p hl j", hl=2), wo16.buf),
            TRef(wosp_d[c].rearrange("hl p j -> p hl j"), dbuf["wsp"]))
    for t in range(NT_Q):
        ai = aoin[t % 2]
        a16 = aoi16[t % 2]
        xr = xres[t % 2]
        for c in range(NPAIR):
            dma(TRef(ai.ap[:, c * 128:(c + 1) * 128], ai.buf),
                dref(ao_d[c][:, t * 128:(t + 1) * 128], "ao"))
        dma(xr, dref(xkv_d[QOFF + t * 128:QOFF + (t + 1) * 128, :], "xkv"))
        split(TRef(a16.ap[:, 0:1024], a16.buf),
              TRef(a16.ap[:, 1024:2048], a16.buf), ai)
        for dh in range(2):
            p = pb[2 * (t % 2) + dh]
            po = TRef(p.ap, p.buf)
            for c in range(DC):
                mm3(po,
                    (TRef(a16.ap[:, c * 128:(c + 1) * 128], a16.buf),
                     TRef(a16.ap[:, 1024 + c * 128:1024 + (c + 1) * 128],
                          a16.buf)),
                    (TRef(wo16.ap[:, c * 2048 + dh * 512:
                                  c * 2048 + (dh + 1) * 512], wo16.buf),
                     TRef(wo16.ap[:, c * 2048 + 1024 + dh * 512:
                                  c * 2048 + 1024 + (dh + 1) * 512], wo16.buf)),
                    start=(c == 0), stop=(c == DC - 1), tag="oproj")
            xs = x2st[t % 2]
            tt(TRef(xs.ap[:, dh * 512:(dh + 1) * 512], xs.buf), po,
               TRef(xr.ap[:, dh * 512:(dh + 1) * 512], xr.buf), ALU.add)
        dma(dref(x2_d[t], "x2"), x2st[t % 2], eng="gpsimd")

    def emit_ffn(instream, inname, outstream, outname, dbg=False):
        P.barrier()
        ar.reset(ov)
        ar16.reset(ov16)
        drain_prep(1000)
        xf16 = [ar16.alloc(f"fxf{hl}", 4 * 1024) for hl in range(2)]
        eh16 = [ar16.alloc(f"feh{i}", 1024) for i in range(NHC)]
        ew1sl = [ar16.alloc(f"few1{i}", 2048) for i in range(2)]
        ew2sl = [ar16.alloc(f"few2{i}", 2048) for i in range(2)]
        rw116 = [ar16.alloc(f"frw1{i}", 2048) for i in range(2)]
        r1T = [ar.alloc(f"fr1T{c}", 512) for c in range(DC)]
        eh32 = [ar.alloc(f"feh32{i}", 512) for i in range(2)]
        ytile = [ar.alloc(f"fy{i}", 1024) for i in range(4)]
        fxin = [ar.alloc(f"fxin{i}", 1024) for i in range(2)]
        fxh = [ar.alloc(f"fxh{i}", 1024) for i in range(2)]
        fst = [ar.alloc(f"fst{i}", 8) for i in range(2)]
        fscr2 = [ar.alloc(f"fscr{i}", 1024) for i in range(2)]
        lt = [ar.alloc(f"flt{i}", 8) for i in range(4)]
        wsb = [ar.alloc(f"fw{i}", 8) for i in range(4)]
        wts = [ar.alloc(f"fwts{i}", 24) for i in range(4)]
        wTs = [ar.alloc(f"fwT{i}", 128) for i in range(4)]
        ost = [ar.alloc(f"fo{i}", 1024) for i in range(2)]
        rb1p = ar.alloc("rb1p", 8)
        rb2r = ar.alloc("rb2r", 8)
        eb1p = ar.alloc("eb1p", 128)
        eb2s = ar.alloc("eb2s", 1024)
        rw2s = ar.alloc("rw2s", 64)
        dma(rb1p, TRef(rb1_d, Buf("rb1d")))
        dma(TRef(rb2r.ap[0:1, :], rb2r.buf), TRef(rb2_d, Buf("rb2d")))
        dma(eb1p, TRef(eb1_d, Buf("eb1d")))
        dma(TRef(eb2s.ap[0:8, :], eb2s.buf), TRef(eb2_d, Buf("eb2d")))
        dma(rw2s, TRef(rw2_d, Buf("rw2d")))

        def xpair(c, nt):
            outs = []
            for hl in range(2):
                v = xf16[hl].ap[:, 0:nt * 1024].rearrange(
                    "p (ti cj) -> p ti cj", cj=1024)
                outs.append(TRef(v[:, :, c * 128:(c + 1) * 128], xf16[hl].buf))
            return tuple(outs)

        def xpair_t(c, ti):
            return (TRef(xf16[0].ap[:, ti * 1024 + c * 128:
                                    ti * 1024 + (c + 1) * 128], xf16[0].buf),
                    TRef(xf16[1].ap[:, ti * 1024 + c * 128:
                                    ti * 1024 + (c + 1) * 128], xf16[1].buf))

        for qi, (qs, qw) in enumerate(QCS):
            nt = qw // 128
            for ti in range(nt):
                t = qs // 128 + ti
                xi = fxin[ti % 2]
                xh = fxh[ti % 2]
                dma(xi, dref(instream[t], inname))
                emit_ln(xi, xh, fst[ti % 2], fscr2[ti % 2])
                tt(ytile[ti], xi, xh, ALU.add)
                for half in range(2):
                    p = pb[(2 * ti + half) % 4]
                    for cc in range(4):
                        c = half * 4 + cc
                        trans(TRef(p.ap[:, cc * 128:(cc + 1) * 128], p.buf),
                              TRef(xh.ap[:, c * 128:(c + 1) * 128], xh.buf))
                    split(TRef(xf16[0].ap[:, ti * 1024 + half * 512:
                                          ti * 1024 + (half + 1) * 512],
                               xf16[0].buf),
                          TRef(xf16[1].ap[:, ti * 1024 + half * 512:
                                          ti * 1024 + (half + 1) * 512],
                               xf16[1].buf),
                          TRef(p.ap[:, 0:512], p.buf))
            for oc in range(DC):
                rw = rw116[oc % 2]
                dma(TRef(rw.ap.rearrange("p (hl j) -> p hl j", hl=2), rw.buf),
                    TRef(rw1sp_d[oc].rearrange("hl p j -> p hl j"),
                         dbuf["wsp"]))
                p = pb[2 + oc % 2]
                po = TRef(p.ap[:, 0:qw], p.buf)
                for c in range(DC):
                    mm3(po,
                        (TRef(rw.ap[:, c * 128:(c + 1) * 128], rw.buf),
                         TRef(rw.ap[:, 1024 + c * 128:1024 + (c + 1) * 128],
                              rw.buf)),
                        xpair(c, nt),
                        start=(c == 0), stop=(c == DC - 1), tag="r1")
                act(TRef(r1T[oc].ap[:, 0:qw], r1T[oc].buf), po, AF.Silu,
                    bias=TRef(rb1p.ap[:, oc:oc + 1], rb1p.buf))
            if dbg and qi < 2:
                dbg_dump(TRef(fxh[0].ap[:, 0:512], fxh[0].buf), w=512)
                dbg_dump(TRef(r1T[0].ap[:, 0:qw], r1T[0].buf), w=qw)
            for ti in range(nt):
                p = pb[4]
                po = TRef(p.ap[:, 0:8], p.buf)
                for oc in range(DC):
                    mm(po, TRef(r1T[oc].ap[:, ti * 128:(ti + 1) * 128],
                                r1T[oc].buf),
                       TRef(rw2s.ap[:, oc * 8:(oc + 1) * 8], rw2s.buf),
                       start=(oc == 0), stop=False)
                mm(po, TRef(ones.ap[0:1, 0:128], ones.buf),
                   TRef(rb2r.ap[0:1, 0:8], rb2r.buf), start=False, stop=True)
                l = lt[ti]
                copy(l, po)
                st = wts[ti]
                m1 = TRef(st.ap[:, 0:1], st.buf)
                nm1 = TRef(st.ap[:, 1:2], st.buf)
                m2 = TRef(st.ap[:, 2:3], st.buf)
                ds_ = TRef(st.ap[:, 3:4], st.buf)
                dr_ = TRef(st.ap[:, 4:5], st.buf)
                eqt = TRef(st.ap[:, 8:16], st.buf)
                l2t = TRef(st.ap[:, 16:24], st.buf)
                red(m1, l, ALU.max)
                ts(nm1, m1, -1.0, ALU.mult)
                ts(eqt, l, m1, ALU.is_ge)
                stt(l2t, eqt, -1e30, l, ALU.mult, ALU.add)
                red(m2, l2t, ALU.max)
                ts(l2t, l, m2, ALU.is_ge)
                act(eqt, l, AF.Exp, bias=nm1)
                tt(eqt, eqt, l2t, ALU.mult)
                red(ds_, eqt, ALU.add)
                recip(dr_, ds_)
                ts(wsb[ti], eqt, dr_, ALU.mult)
                p2 = pb[5]
                pw = TRef(p2.ap[0:8, 0:128], p2.buf)
                trans(pw, wsb[ti])
                copy(TRef(wTs[ti].ap[0:8, :], wTs[ti].buf), pw)
            if dbg and qi < 2:
                dbg_dump(lt[0], w=8)
                dbg_dump(wsb[0], w=8)
            for ei in range(E):
                for hc in range(NHC):
                    w1 = ew1sl[hc % 2]
                    dma(TRef(w1.ap.rearrange("p (hl j) -> p hl j", hl=2),
                             w1.buf),
                        TRef(ew1sp_d[ei][:, :, hc * 1024:(hc + 1) * 1024]
                             .rearrange("hl p j -> p hl j"), dbuf["wsp"]),
                        tag="ew1")
                    p = pb[2 + hc % 2]
                    po = TRef(p.ap[:, 0:qw], p.buf)
                    for c in range(DC):
                        mm3(po,
                            (TRef(w1.ap[:, c * 128:(c + 1) * 128], w1.buf),
                             TRef(w1.ap[:, 1024 + c * 128:1024 + (c + 1) * 128],
                                  w1.buf)),
                            xpair(c, nt),
                            start=(c == 0), stop=(c == DC - 1), tag="e1")
                    e32 = eh32[hc % 2]
                    act(TRef(e32.ap[:, 0:qw], e32.buf), po, AF.Silu,
                        bias=TRef(eb1p.ap[:, ei * 16 + hc:ei * 16 + hc + 1],
                                  eb1p.buf), tag="silu1")
                    split(TRef(eh16[hc].ap[:, 0:qw], eh16[hc].buf),
                          TRef(eh16[hc].ap[:, 512:512 + qw], eh16[hc].buf),
                          TRef(e32.ap[:, 0:qw], e32.buf))
                for hc in range(NHC):
                    e2w = ew2sl[hc % 2]
                    dma(TRef(e2w.ap.rearrange("p (hl j) -> p hl j", hl=2),
                             e2w.buf),
                        TRef(ew2sp_d[ei, hc].rearrange("hl p j -> p hl j"),
                             dbuf["wsp"]), tag="ew2")
                    for ti in range(nt):
                        ehp = (TRef(eh16[hc].ap[:, ti * 128:(ti + 1) * 128],
                                    eh16[hc].buf),
                               TRef(eh16[hc].ap[:, 512 + ti * 128:
                                                512 + (ti + 1) * 128],
                                    eh16[hc].buf))
                        for dh in range(2):
                            p = pb[ti * 2 + dh]
                            po = TRef(p.ap, p.buf)
                            mm3(po, ehp,
                                (TRef(e2w.ap[:, dh * 512:(dh + 1) * 512],
                                      e2w.buf),
                                 TRef(e2w.ap[:, 1024 + dh * 512:
                                             1024 + (dh + 1) * 512], e2w.buf)),
                                start=(hc == 0), stop=(hc == NHC - 1), tag="e2")
                order = [(1, 0), (1, 1), (0, 0), (0, 1)] if nt > 1 else [(0, 0), (0, 1)]
                order += [(ti, dh) for ti in range(2, nt) for dh in range(2)]
                for (ti, dh) in order:
                    p = pb[ti * 2 + dh]
                    stt(TRef(ytile[ti].ap[:, dh * 512:(dh + 1) * 512],
                             ytile[ti].buf),
                        TRef(p.ap, p.buf),
                        TRef(wsb[ti].ap[:, ei:ei + 1], wsb[ti].buf),
                        TRef(ytile[ti].ap[:, dh * 512:(dh + 1) * 512],
                             ytile[ti].buf),
                        ALU.mult, ALU.add, tag="e2evac")
                if dbg and qi < 2 and ei == 0:
                    dbg_dump(TRef(eh32[0].ap[:, 0:qw], eh32[0].buf), w=qw)
                    dbg_dump(TRef(ytile[0].ap[:, 0:512], ytile[0].buf), w=512)
            for ti in range(nt):
                t = qs // 128 + ti
                o = ost[ti % 2]
                for dh in range(2):
                    p = pb[4 + dh]
                    po = TRef(p.ap, p.buf)
                    mm(po, TRef(wTs[ti].ap[0:8, :], wTs[ti].buf),
                       TRef(eb2s.ap[0:8, dh * 512:(dh + 1) * 512], eb2s.buf),
                       start=True, stop=True, tag="eb2")
                    tt(TRef(o.ap[:, dh * 512:(dh + 1) * 512], o.buf),
                       TRef(ytile[ti].ap[:, dh * 512:(dh + 1) * 512], ytile[ti].buf),
                       po, ALU.add)
                dma(dref(outstream[t], outname), o, eng="gpsimd")
            if dbg and qi < 2:
                dbg_dump(TRef(ytile[0].ap[:, 0:512], ytile[0].buf), w=512)


    posI_t = [nc.alloc_sbuf_tensor(f"posI{i}", [128, 8], mybir.dt.int32).ap()
              for i in range(2)]
    destI4_t = [nc.alloc_sbuf_tensor(f"destI{i}", [128, 1],
                                     mybir.dt.int32).ap() for i in range(6)]
    tokI_t = [nc.alloc_sbuf_tensor(f"tokI{i}", [128, 1], mybir.dt.int32).ap()
              for i in range(4)]
    wIdxA_t = nc.alloc_sbuf_tensor("wIdxA", [128, NSLOT], mybir.dt.int32).ap()
    wIdx16_t = [nc.alloc_sbuf_tensor(f"wIdx16_{i}", [128, 16],
                                     mybir.dt.int32).ap() for i in range(2)]
    segI_t = nc.alloc_sbuf_tensor("segI", [1, 8], mybir.dt.int32).ap()
    posI = [TRef(a, Buf(f"posI{i}")) for i, a in enumerate(posI_t)]
    destI4 = [TRef(a, Buf(f"destI{i}")) for i, a in enumerate(destI4_t)]
    tokI = [TRef(a, Buf(f"tokI{i}")) for i, a in enumerate(tokI_t)]
    wIdxA = TRef(wIdxA_t, Buf("wIdxA"))
    wIdx16 = [TRef(a, Buf(f"wIdx16_{i}")) for i, a in enumerate(wIdx16_t)]
    segI = TRef(segI_t, Buf("segI"))

    def idma_gather(out, in_, idx, tag="ig"):
        P.emit("gpsimd",
               (lambda e, o=out.ap, i=in_.ap, x=idx.ap:
                e.indirect_dma_start(out=o, out_offset=None, in_=i,
                                     in_offset=IndirectOffsetOnAxis(x, 0))),
               [in_.buf, idx.buf], [out.buf], dma=True, tag=tag)

    def idma_scatter(out, in_, idx, tag="is"):
        P.emit("gpsimd",
               (lambda e, o=out.ap, i=in_.ap, x=idx.ap:
                e.indirect_dma_start(out=o, out_offset=IndirectOffsetOnAxis(x, 0),
                                     in_=i, in_offset=None)),
               [in_.buf, idx.buf], [out.buf], dma=True, tag=tag)

    def emit_ffn_sparse(instream, inname, outstream, outname):
        P.barrier()
        ar.reset(ov)
        ar16.reset(ov16)
        drain_prep(1000)
        xf16 = [ar16.alloc(f"sxf{hl}", 4 * 1024) for hl in range(2)]
        gx16 = [ar16.alloc(f"sgx{hl}", 2048) for hl in range(2)]
        eh16 = [ar16.alloc(f"seh{i}", 512) for i in range(NHC)]
        ew1sl = [ar16.alloc(f"sew1{i}", 2048) for i in range(3)]
        ew2sl = [ar16.alloc(f"sew2{i}", 2048) for i in range(3)]
        rw116 = [ar16.alloc(f"srw1{i}", 2048) for i in range(2)]
        fxin = [ar.alloc(f"sfxin{i}", 1024) for i in range(2)]
        fxh = [ar.alloc(f"sfxh{i}", 1024) for i in range(2)]
        fst = [ar.alloc(f"sfst{i}", 8) for i in range(2)]
        fscr2 = [ar.alloc(f"sfscr{i}", 1024) for i in range(2)]
        wS = ar.alloc("swS", NT_Q * 8)
        rankS = ar.alloc("srankS", NT_Q * 8)
        jS = ar.alloc("sjS", NT_Q * 8)
        wTsS = ar.alloc("swTsS", TQ)
        baseR = ar.alloc("sbaseR", 8)
        segrow = ar.alloc("ssegrow", 8)
        eslB = ar.alloc("seslB", NSLOT)
        tokfS = ar.alloc("stokf", NT_Q)
        iot34 = ar.alloc("siot34", NSLOT)
        segtmp = ar.alloc("ssegtmp", 80)
        msb = ar.alloc("smsb", 128)
        u8sb = ar.alloc("su8", 8)
        onec = ar.alloc("sonec", 8)
        rb1p = ar.alloc("rb1p", 8)
        rb2r = ar.alloc("rb2r", 8)
        eb1p = ar.alloc("eb1p", 128)
        eb2s = ar.alloc("eb2s", 1024)
        rw2s = ar.alloc("rw2s", 64)
        rfill = ar.alloc("srfill", NS2 * 8)
        hcrow = ar.alloc("shcrow", 16)
        ovS = ar.mark()
        r1T = [ar.alloc(f"sr1T{c}", 512) for c in range(DC)]
        lt = [ar.alloc(f"slt{i}", 8) for i in range(4)]
        wts = [ar.alloc(f"swts{i}", 56) for i in range(4)]
        pay = [ar.alloc(f"spay{i}", 64) for i in range(2)]
        posf = ar.alloc("sposf", 8)
        destf = ar.alloc("sdestf", 8)
        tokrep = ar.alloc("stokrep", 8)
        ind3S = ar.alloc("sind3S", NT_Q * 8)
        ind3T = [ar.alloc(f"sind3T{i}", 128) for i in range(2)]
        dma(rb1p, TRef(rb1_d, Buf("rb1d")))
        dma(TRef(rb2r.ap[0:1, :], rb2r.buf), TRef(rb2_d, Buf("rb2d")))
        dma(eb1p, TRef(eb1_d, Buf("eb1d")))
        dma(TRef(eb2s.ap[0:8, :], eb2s.buf), TRef(eb2_d, Buf("eb2d")))
        dma(rw2s, TRef(rw2_d, Buf("rw2d")))
        dma(TRef(msb.ap, msb.buf), TRef(mstrict_d, Buf("msd")))
        dma(TRef(u8sb.ap[0:8, :], u8sb.buf), TRef(u8_d, Buf("u8d")))
        dma(TRef(iot34.ap[0:1, :], iot34.buf), TRef(iota34_d, Buf("i34d")))
        dma(tokfS, TRef(tokf_d, Buf("tokfd")))
        dma(rfill, TRef(rtfill_d, Buf("rtfd")))
        ts(TRef(hcrow.ap, hcrow.buf), TRef(tokfS.ap[:, 0:16], tokfS.buf),
           TRef(tokfS.ap[:, 0:1], tokfS.buf), ALU.subtract, 8.0, ALU.mult)
        memset(TRef(baseR.ap[0:1, :], baseR.buf), 0.0)
        memset(onec, 1.0)
        dma(TRef(rt_d[0:RTROWS].rearrange("(s p) k -> p s k", p=128),
                 dbuf["rt"]),
            TRef(rfill.ap.rearrange("p (s k) -> p s k", k=8), rfill.buf),
            eng="gpsimd")

        def xpair(c, nt):
            outs = []
            for hl in range(2):
                v = xf16[hl].ap[:, 0:nt * 1024].rearrange(
                    "p (ti cj) -> p ti cj", cj=1024)
                outs.append(TRef(v[:, :, c * 128:(c + 1) * 128], xf16[hl].buf))
            return tuple(outs)

        for qi, (qs, qw) in enumerate(QCS):
            nt = qw // 128
            for ti in range(nt):
                t = qs // 128 + ti
                xi = fxin[ti % 2]
                xh = fxh[ti % 2]
                dma(xi, dref(instream[t], inname))
                emit_ln(xi, xh, fst[ti % 2], fscr2[ti % 2])
                dma(TRef(xhtm_d[t], dbuf["xhtm"]), xh, eng="gpsimd")
                for half in range(2):
                    p = pb[(2 * ti + half) % 4]
                    for cc in range(4):
                        c = half * 4 + cc
                        trans(TRef(p.ap[:, cc * 128:(cc + 1) * 128], p.buf),
                              TRef(xh.ap[:, c * 128:(c + 1) * 128], xh.buf))
                    split(TRef(xf16[0].ap[:, ti * 1024 + half * 512:
                                          ti * 1024 + (half + 1) * 512],
                               xf16[0].buf),
                          TRef(xf16[1].ap[:, ti * 1024 + half * 512:
                                          ti * 1024 + (half + 1) * 512],
                               xf16[1].buf),
                          TRef(p.ap[:, 0:512], p.buf))
            for oc in range(DC):
                rw = rw116[oc % 2]
                dma(TRef(rw.ap.rearrange("p (hl j) -> p hl j", hl=2), rw.buf),
                    TRef(rw1sp_d[oc].rearrange("hl p j -> p hl j"),
                         dbuf["wsp"]))
                p = pb[4 + oc % 2]
                po = TRef(p.ap[:, 0:qw], p.buf)
                for c in range(DC):
                    mm3(po,
                        (TRef(rw.ap[:, c * 128:(c + 1) * 128], rw.buf),
                         TRef(rw.ap[:, 1024 + c * 128:1024 + (c + 1) * 128],
                              rw.buf)),
                        xpair(c, nt),
                        start=(c == 0), stop=(c == DC - 1), tag="r1")
                act(TRef(r1T[oc].ap[:, 0:qw], r1T[oc].buf), po, AF.Silu,
                    bias=TRef(rb1p.ap[:, oc:oc + 1], rb1p.buf))
            for ti in range(nt):
                t = qs // 128 + ti
                p = pb[6]
                po = TRef(p.ap[:, 0:8], p.buf)
                for oc in range(DC):
                    mm(po, TRef(r1T[oc].ap[:, ti * 128:(ti + 1) * 128],
                                r1T[oc].buf),
                       TRef(rw2s.ap[:, oc * 8:(oc + 1) * 8], rw2s.buf),
                       start=(oc == 0), stop=False)
                mm(po, TRef(ones.ap[0:1, 0:128], ones.buf),
                   TRef(rb2r.ap[0:1, 0:8], rb2r.buf), start=False, stop=True)
                l = lt[ti]
                copy(l, po)
                st = wts[ti]
                m1 = TRef(st.ap[:, 0:1], st.buf)
                nm1 = TRef(st.ap[:, 1:2], st.buf)
                m2 = TRef(st.ap[:, 2:3], st.buf)
                ds_ = TRef(st.ap[:, 3:4], st.buf)
                dr_ = TRef(st.ap[:, 4:5], st.buf)
                m3 = TRef(st.ap[:, 5:6], st.buf)
                eq1 = TRef(st.ap[:, 8:16], st.buf)
                l2t = TRef(st.ap[:, 16:24], st.buf)
                eq2 = TRef(st.ap[:, 24:32], st.buf)
                l3t = TRef(st.ap[:, 32:40], st.buf)
                emt = TRef(st.ap[:, 40:48], st.buf)
                red(m1, l, ALU.max)
                ts(nm1, m1, -1.0, ALU.mult)
                ts(eq1, l, m1, ALU.is_ge)
                stt(l2t, eq1, -1e30, l, ALU.mult, ALU.add)
                red(m2, l2t, ALU.max)
                ts(eq2, l2t, m2, ALU.is_ge)
                stt(l3t, eq2, -1e30, l2t, ALU.mult, ALU.add)
                red(m3, l3t, ALU.max)
                ind3 = TRef(ind3S.ap[:, t * 8:(t + 1) * 8], ind3S.buf)
                ts(ind3, l, m3, ALU.is_ge)
                ts(eq2, l, m2, ALU.is_ge)
                act(emt, l, AF.Exp, bias=nm1)
                tt(emt, emt, eq2, ALU.mult)
                red(ds_, emt, ALU.add)
                recip(dr_, ds_)
                wsl_ = TRef(wS.ap[:, t * 8:(t + 1) * 8], wS.buf)
                ts(wsl_, emt, dr_, ALU.mult)
                pw = TRef(pb[7].ap[0:8, 0:128], pb[7].buf)
                trans(pw, wsl_)
                copy(TRef(wTsS.ap[0:8, t * 128:(t + 1) * 128], wTsS.buf), pw)
                pr_ = TRef(pb[7].ap[:, 128:136], pb[7].buf)
                mm(pr_, TRef(msb.ap, msb.buf), ind3, start=True, stop=False,
                   tag="rank")
                mm(pr_, TRef(ones.ap[0:1, 0:128], ones.buf),
                   TRef(baseR.ap[0:1, 0:8], baseR.buf), start=False, stop=True,
                   tag="rankb")
                copy(TRef(rankS.ap[:, t * 8:(t + 1) * 8], rankS.buf), pr_)
                pc_ = TRef(pb[7].ap[0:1, 136:144], pb[7].buf)
                mm(pc_, TRef(onec.ap[:, 0:1], onec.buf), ind3,
                   start=True, stop=True, tag="cnt")
                tt(TRef(baseR.ap[0:1, 0:8], baseR.buf),
                   TRef(baseR.ap[0:1, 0:8], baseR.buf), pc_, ALU.add)
                pi_ = TRef(pb[6].ap[0:8, 0:128], pb[6].buf)
                trans(pi_, ind3)
                it_ = ind3T[ti % 2]
                copy(TRef(it_.ap[0:8, :], it_.buf), pi_)
                pj_ = TRef(pb[6].ap[:, 256:264], pb[6].buf)
                mm(pj_, TRef(it_.ap[0:8, :], it_.buf),
                   TRef(u8sb.ap[0:8, 0:8], u8sb.buf), start=True, stop=True,
                   tag="jmm")
                copy(TRef(jS.ap[:, t * 8:(t + 1) * 8], jS.buf), pj_)
        a_ = TRef(segtmp.ap[0:1, 0:8], segtmp.buf)
        ts(a_, TRef(baseR.ap[0:1, 0:8], baseR.buf), 255.0, ALU.add)
        f_ = TRef(segtmp.ap[0:1, 8:16], segtmp.buf)
        fi_ = TRef(segtmp.ap[0:1, 56:64], segtmp.buf)
        g_ = TRef(segtmp.ap[0:1, 64:72], segtmp.buf)
        ts(f_, a_, 1.0 / 256.0, ALU.mult)
        copy(TRef(segI_t, segI.buf), f_)
        copy(fi_, TRef(segI_t, segI.buf))
        tt(g_, fi_, f_, ALU.is_gt)
        tt(fi_, fi_, g_, ALU.subtract)
        ts(a_, fi_, 256.0, ALU.mult)
        r2_ = TRef(segrow.ap[0:1, 0:8], segrow.buf)
        pp_ = TRef(pb[0].ap[0:8, 0:1], pb[0].buf)
        mm(pp_, a_, TRef(ones.ap[0:1, 0:1], ones.buf),
           start=True, stop=True, tag="segT")
        pf_ = TRef(segtmp.ap[0:8, 16:17], segtmp.buf)
        copy(pf_, pp_)
        ps_ = TRef(pb[0].ap[0:1, 8:16], pb[0].buf)
        mm(ps_, pf_, TRef(u8sb.ap[0:8, 0:8], u8sb.buf), start=True, stop=True,
           tag="seg")
        copy(r2_, ps_)
        st_ = TRef(segtmp.ap[0:1, 24:32], segtmp.buf)
        ts(st_, r2_, 1.0 / 256.0, ALU.mult)
        es1 = TRef(segtmp.ap[0:1, 32:33], segtmp.buf)
        memset(es1, -1.0)
        esacc = TRef(segtmp.ap[0:1, 40:40 + NSLOT], segtmp.buf)
        memset(esacc, -1.0)
        for e in range(E):
            stt(esacc, TRef(iot34.ap[0:1, :], iot34.buf),
                TRef(st_.ap[0:1, e:e + 1], segtmp.buf), esacc,
                ALU.is_ge, ALU.add)
        pe_ = TRef(pb[0].ap[:, 16:16 + NSLOT], pb[0].buf)
        mm(pe_, TRef(ones.ap[0:1, 0:128], ones.buf), esacc,
           start=True, stop=True, tag="eslb")
        copy(eslB, pe_)
        pg_ = TRef(pb[1].ap[:, 0:8], pb[1].buf)
        mm(pg_, TRef(ones.ap[0:1, 0:128], ones.buf), r2_,
           start=True, stop=True, tag="segb")
        segB = TRef(segtmp.ap[:, 48:56], segtmp.buf)
        copy(segB, pg_)
        for py_ in pay:
            memset(TRef(py_.ap, py_.buf), 0.0)
        for t in range(NT_Q):
            pf = TRef(posf.ap, posf.buf)
            tt(pf, TRef(rankS.ap[:, t * 8:(t + 1) * 8], rankS.buf), segB,
               ALU.add)
            ts(pf, pf, float(RTROWS), ALU.subtract)
            tt(pf, pf, TRef(ind3S.ap[:, t * 8:(t + 1) * 8], ind3S.buf),
               ALU.mult)
            ts(pf, pf, float(RTROWS), ALU.add)
            pI = posI[t % 2]
            copy(TRef(posI_t[t % 2], pI.buf), pf)
            jd = TRef(destf.ap, destf.buf)
            ts(jd, TRef(jS.ap[:, t * 8:(t + 1) * 8], jS.buf), 4.0, ALU.min)
            ts(jd, jd, float(TQ), ALU.mult,
               TRef(tokfS.ap[:, t:t + 1], tokfS.buf), ALU.add)
            tr_ = TRef(tokrep.ap, tokrep.buf)
            ts(tr_, TRef(wS.ap[:, t * 8:(t + 1) * 8], wS.buf), 0.0,
               ALU.mult, TRef(tokfS.ap[:, t:t + 1], tokfS.buf), ALU.add)
            py = pay[t % 2]
            copy(TRef(py.ap[:, 0:64:8], py.buf), tr_)
            copy(TRef(py.ap[:, 1:64:8], py.buf),
                 TRef(wS.ap[:, t * 8:(t + 1) * 8], wS.buf))
            copy(TRef(py.ap[:, 2:64:8], py.buf), jd)
            for e in range(E):
                idma_scatter(TRef(rt_d, dbuf["rt"]),
                             TRef(py.ap[:, e * 8:(e + 1) * 8], py.buf),
                             TRef(posI_t[t % 2][:, e:e + 1], pI.buf), tag="rtsc")
        P.barrier()
        ar.reset(ovS)
        eh32 = [ar.alloc(f"seh32{i}", 256) for i in range(2)]
        ebcols = [ar.alloc(f"sebc{i}", 16) for i in range(2)]
        rts = [ar.alloc(f"srts{i}", 8) for i in range(4)]
        gxh = [ar.alloc(f"sgxh{i}", 1024) for i in range(4)]
        eo_sb = [ar.alloc(f"seo{i}", 1024) for i in range(4)]
        acc = [ar.alloc(f"sacc{i}", 1024) for i in range(2)]
        ldb = [ar.alloc(f"sldb{i}", 1024) for i in range(2)]
        ost = [ar.alloc(f"sost{i}", 1024) for i in range(2)]
        wiAll = ar.alloc("swiA", NSLOT)
        wi16All = ar.alloc("swi16A", NSLOT * 16)
        ts(TRef(wiAll.ap, wiAll.buf), TRef(eslB.ap, eslB.buf), 128.0,
           ALU.mult, TRef(tokfS.ap[:, 0:1], tokfS.buf), ALU.add)
        copy(TRef(wIdxA_t, wIdxA.buf), TRef(wiAll.ap, wiAll.buf))
        for s in range(NSLOT):
            ts(TRef(wi16All.ap[:, s * 16:(s + 1) * 16], wi16All.buf),
               TRef(hcrow.ap, hcrow.buf),
               TRef(wiAll.ap[:, s:s + 1], wiAll.buf), ALU.add)

        def slot_wconv(s):
            copy(TRef(wIdx16_t[s % 2], wIdx16[s % 2].buf),
                 TRef(wi16All.ap[:, s * 16:(s + 1) * 16], wi16All.buf))

        def slot_rt(s):
            for sub in range(2):
                b = (2 * s + sub) % 4
                b6 = (2 * s + sub) % 6
                rts_ = rts[b]
                dma(rts_, TRef(rt_d[s * 256 + sub * 128:
                                    s * 256 + sub * 128 + 128, :], dbuf["rt"]))
                copy(TRef(tokI_t[b], tokI[b].buf),
                     TRef(rts_.ap[:, 0:1], rts_.buf))
                copy(TRef(destI4_t[b6], destI4[b6].buf),
                     TRef(rts_.ap[:, 2:3], rts_.buf))

        def slot_gx(s):
            for sub in range(2):
                b = (2 * s + sub) % 4
                idma_gather(gxh[b], TRef(xhtm_d.rearrange("t p j -> (t p) j"),
                                         dbuf["xhtm"]), tokI[b], tag="gxh")
            idma_gather(ebcols[s % 2],
                        TRef(eb1g_d.rearrange("e p h -> (e p) h"),
                             Buf("eb1gd")),
                        TRef(wIdxA_t[:, s:s + 1], wIdxA.buf), tag="geb1")

        def slot_scatter(s):
            for sub in range(2):
                b = (2 * s + sub) % 4
                b6 = (2 * s + sub) % 6
                idma_scatter(TRef(ysl_d, dbuf[f"ysl{s % 4}"]), eo_sb[b],
                             destI4[b6], tag="eosc")

        def gxp(c):
            outs = []
            for hl in range(2):
                v = gx16[hl].ap.rearrange("p (ti cj) -> p ti cj", cj=1024)
                outs.append(TRef(v[:, :, c * 128:(c + 1) * 128],
                                 gx16[hl].buf))
            return tuple(outs)

        slot_rt(0)
        slot_wconv(0)
        slot_gx(0)
        for s in range(NSLOT):
            if s + 1 < NSLOT:
                slot_rt(s + 1)
                slot_wconv(s + 1)
            for sub in range(2):
                b = (2 * s + sub) % 4
                for half in range(2):
                    p = pb[half]
                    for cc in range(4):
                        c = half * 4 + cc
                        trans(TRef(p.ap[:, cc * 128:(cc + 1) * 128], p.buf),
                              TRef(gxh[b].ap[:, c * 128:(c + 1) * 128],
                                   gxh[b].buf))
                    split(TRef(gx16[0].ap[:, sub * 1024 + half * 512:
                                          sub * 1024 + (half + 1) * 512],
                               gx16[0].buf),
                          TRef(gx16[1].ap[:, sub * 1024 + half * 512:
                                          sub * 1024 + (half + 1) * 512],
                               gx16[1].buf),
                          TRef(p.ap[:, 0:512], p.buf))
            if s + 1 < NSLOT:
                slot_gx(s + 1)
            def gath_w1(hc):
                w1 = ew1sl[(s * 16 + hc) % 3]
                idma_gather(w1,
                            TRef(ew1sp_d.rearrange(
                                "c e p hl j -> (c e p) hl j"),
                                 dbuf["wsp"]),
                            TRef(wIdx16_t[s % 2][:, hc:hc + 1],
                                 wIdx16[s % 2].buf), tag="gw1")

            def gath_w2(hc):
                e2w = ew2sl[(s * 16 + hc) % 3]
                idma_gather(e2w,
                            TRef(ew2sp_d.rearrange(
                                "c e p hl j -> (c e p) hl j"), dbuf["wsp"]),
                            TRef(wIdx16_t[s % 2][:, hc:hc + 1],
                                 wIdx16[s % 2].buf), tag="gw2")

            gath_w1(0)
            gath_w1(1)
            for hc in range(NHC):
                if hc + 2 < NHC:
                    gath_w1(hc + 2)
                w1 = ew1sl[(s * 16 + hc) % 3]
                p = pb[2 + hc % 2]
                po = TRef(p.ap[:, 0:256], p.buf)
                for c in range(DC):
                    mm3(po,
                        (TRef(w1.ap[:, c * 128:(c + 1) * 128], w1.buf),
                         TRef(w1.ap[:, 1024 + c * 128:1024 + (c + 1) * 128],
                              w1.buf)),
                        gxp(c), start=(c == 0), stop=(c == DC - 1), tag="se1")
                e32 = eh32[hc % 2]
                act(TRef(e32.ap, e32.buf), po, AF.Silu,
                    bias=TRef(ebcols[s % 2].ap[:, hc:hc + 1],
                              ebcols[s % 2].buf), tag="silu1x")
                split(TRef(eh16[hc].ap[:, 0:256], eh16[hc].buf),
                      TRef(eh16[hc].ap[:, 256:512], eh16[hc].buf),
                      TRef(e32.ap, e32.buf))
                if hc >= NHC - 2:
                    gath_w2(hc - (NHC - 2))
            for hc in range(NHC):
                if hc + 2 < NHC:
                    gath_w2(hc + 2)
                e2w = ew2sl[(s * 16 + hc) % 3]
                for sub in range(2):
                    ehp = (TRef(eh16[hc].ap[:, sub * 128:(sub + 1) * 128],
                                eh16[hc].buf),
                           TRef(eh16[hc].ap[:, 256 + sub * 128:
                                            256 + (sub + 1) * 128],
                                eh16[hc].buf))
                    for dh in range(2):
                        p = pb[4 + sub * 2 + dh]
                        po = TRef(p.ap, p.buf)
                        mm3(po, ehp,
                            (TRef(e2w.ap[:, dh * 512:(dh + 1) * 512], e2w.buf),
                             TRef(e2w.ap[:, 1024 + dh * 512:
                                         1024 + (dh + 1) * 512], e2w.buf)),
                            start=(hc == 0), stop=(hc == NHC - 1), tag="se2")
            for sub in range(2):
                b = (2 * s + sub) % 4
                eo = eo_sb[b]
                for dh in range(2):
                    p = pb[4 + sub * 2 + dh]
                    ts(TRef(eo.ap[:, dh * 512:(dh + 1) * 512], eo.buf),
                       TRef(p.ap, p.buf),
                       TRef(rts[b].ap[:, 1:2], rts[b].buf), ALU.mult)
            if s >= 1:
                slot_scatter(s - 1)
        slot_scatter(NSLOT - 1)
        yslbufs = [dbuf[f"ysl{i}"] for i in range(4)]
        ldb4 = [ldb[0], ldb[1], fxh[0]]
        for t in range(NT_Q):
            a = acc[t % 2]
            xi = fxin[t % 2]
            dma(xi, dref(instream[t], inname))
            for j in range(3):
                P.emit("sync",
                       (lambda e, o=ldb4[j].ap, i=ysl_d[j * TQ + t * 128:
                                                        j * TQ + (t + 1) * 128, :]:
                        e.dma_start(out=o, in_=i)),
                       [b_ for b_ in yslbufs], [ldb4[j].buf], dma=True, tag="yld")
            dma(eo_sb[t % 2], TRef(xhtm_d[t], dbuf["xhtm"]))
            tt(a, xi, eo_sb[t % 2], ALU.add)
            for j in range(3):
                tt(a, a, ldb4[j], ALU.add)
            o = ost[t % 2]
            for dh in range(2):
                po = TRef(pb[2 + 2 * (t % 2) + dh].ap,
                          pb[2 + 2 * (t % 2) + dh].buf)
                mm(po, TRef(wTsS.ap[0:8, t * 128:(t + 1) * 128], wTsS.buf),
                   TRef(eb2s.ap[0:8, dh * 512:(dh + 1) * 512], eb2s.buf),
                   start=True, stop=True, tag="eb2")
                tt(TRef(o.ap[:, dh * 512:(dh + 1) * 512], o.buf),
                   TRef(a.ap[:, dh * 512:(dh + 1) * 512], a.buf),
                   po, ALU.add)
            dma(dref(outstream[t], outname), o, eng="gpsimd")


    emit_ffn_sparse(x2_d, "x2", x3_d, "x3")

    if STOP_AFTER_FFN1:
        P.barrier()
        P.materialize(stack)
        stack.close()
        return nc

    P.barrier()
    ar.reset(ov)
    ar16.reset(ov16)
    xb16 = [ar16.alloc(f"bxf{hl}", NT_Q * 1024) for hl in range(2)]
    bwsl = [ar.alloc(f"bws{i}", 1024) for i in range(2)]
    bw16 = ar16.alloc("bw16", 2048)
    bxin = [ar.alloc(f"bxi{i}", 1024) for i in range(2)]
    bxh = [ar.alloc(f"bxh{i}", 1024) for i in range(2)]
    bst = [ar.alloc(f"bst{i}", 8) for i in range(2)]
    bscr2 = [ar.alloc(f"bscr{i}", 1024) for i in range(2)]
    bsq = [ar.alloc(f"bsq{i}", 128) for i in range(4)]
    bqn = [ar.alloc(f"bqn{i}", 128) for i in range(4)]
    bfmh = [ar16.alloc(f"bfmh{i}", 128) for i in range(4)]
    bfml = [ar16.alloc(f"bfml{i}", 128) for i in range(4)]
    bvs = [ar.alloc(f"bvs{i}", 130) for i in range(4)]
    bnrm = [ar.alloc(f"bnr{i}", 8) for i in range(4)]

    def x3pair(c, t):
        return (TRef(xb16[0].ap[:, t * 1024 + c * 128:t * 1024 + (c + 1) * 128],
                     xb16[0].buf),
                TRef(xb16[1].ap[:, t * 1024 + c * 128:t * 1024 + (c + 1) * 128],
                     xb16[1].buf))

    for t in range(NT_Q):
        xi = bxin[t % 2]
        xh = bxh[t % 2]
        dma(xi, dref(x3_d[t], "x3"))
        emit_ln(xi, xh, bst[t % 2], bscr2[t % 2])
        for half in range(2):
            p = pb[(2 * t + half) % 4]
            for cc in range(4):
                c = half * 4 + cc
                trans(TRef(p.ap[:, cc * 128:(cc + 1) * 128], p.buf),
                      TRef(xh.ap[:, c * 128:(c + 1) * 128], xh.buf))
            split(TRef(xb16[0].ap[:, t * 1024 + half * 512:
                                  t * 1024 + (half + 1) * 512], xb16[0].buf),
                  TRef(xb16[1].ap[:, t * 1024 + half * 512:
                                  t * 1024 + (half + 1) * 512], xb16[1].buf),
                  TRef(p.ap[:, 0:512], p.buf))

    for _b in bvs:
        memset(TRef(_b.ap[:, 64:65], _b.buf), 1.0)
        memset(TRef(_b.ap[:, 129:130], _b.buf), 1.0)

    for widx, (wdev, outdram, oname, do_norm) in enumerate(
            [(bwq_d, (qbh_d, qbl_d), ("qbh", "qbl"), True),
             (bwk_d, (kbh_d, kbl_d), ("kbh", "kbl"), True),
             (bwv_d, vb_d, "vb", False)]):
        for pr in range(NPAIR):
            wb = bwsl[pr % 2]
            dma(wb, TRef(wdev[pr], Buf("bwd")))
            split(TRef(bw16.ap[:, 0:1024], bw16.buf),
                  TRef(bw16.ap[:, 1024:2048], bw16.buf), wb)

            def bfinish(t, _pr=pr, _out=outdram, _on=oname):
                qn = bqn[t % 4]
                p2 = pb[5 + t % 3]
                pt = TRef(p2.ap[:, 0:128], p2.buf)
                trans(pt, qn)
                fmh = bfmh[t % 4]
                fml = bfml[t % 4]
                copy(fmh, pt, eng="scalar")
                tt(fml, pt, fmh, ALU.subtract)
                dma(dref(_out[0][_pr][:, t * 128:(t + 1) * 128], _on[0]),
                    fmh, eng="gpsimd")
                dma(dref(_out[1][_pr][:, t * 128:(t + 1) * 128], _on[1]),
                    fml, eng="gpsimd")

            for t in range(NT_Q):
                p = pb[2 + t % 3]
                po = TRef(p.ap[:, 0:128], p.buf)
                for c in range(DC):
                    xh_, xl_ = x3pair(c, t)
                    wh_ = TRef(bw16.ap[:, c * 128:(c + 1) * 128], bw16.buf)
                    wl_ = TRef(bw16.ap[:, 1024 + c * 128:1024 + (c + 1) * 128],
                               bw16.buf)
                    for i, (a, b_) in enumerate([(xh_, wh_), (xh_, wl_),
                                                 (xl_, wh_)]):
                        mm(po, a, b_, start=(c == 0 and i == 0),
                           stop=(c == DC - 1 and i == 2), tag="bproj")
                if do_norm:
                    st = bnrm[t % 4]
                    ss0 = TRef(st.ap[:, 0:1], st.buf)
                    ss1 = TRef(st.ap[:, 1:2], st.buf)
                    r0 = TRef(st.ap[:, 2:3], st.buf)
                    r1 = TRef(st.ap[:, 3:4], st.buf)
                    sq = bsq[t % 4]
                    act(TRef(sq.ap[:, 0:64], sq.buf),
                        TRef(po.ap[:, 0:64], p.buf), AF.Square, accum=ss0)
                    act(TRef(sq.ap[:, 64:128], sq.buf),
                        TRef(po.ap[:, 64:128], p.buf), AF.Square, accum=ss1)
                    act(r0, ss0, AF.Sqrt)
                    act(r1, ss1, AF.Sqrt)
                    ts(r0, r0, 1e-6, ALU.max)
                    ts(r1, r1, 1e-6, ALU.max)
                    recip(r0, r0)
                    recip(r1, r1)
                    qn = bqn[t % 4]
                    ts(TRef(qn.ap[:, 0:64], qn.buf), TRef(po.ap[:, 0:64], p.buf),
                       r0, ALU.mult)
                    ts(TRef(qn.ap[:, 64:128], qn.buf), TRef(po.ap[:, 64:128], p.buf),
                       r1, ALU.mult)
                    if t >= 1:
                        bfinish(t - 1)
                    if t == NT_Q - 1:
                        bfinish(t)
                else:
                    vs = bvs[t % 4]
                    copy(TRef(vs.ap[:, 0:64], vs.buf), TRef(po.ap[:, 0:64], p.buf))
                    copy(TRef(vs.ap[:, 65:129], vs.buf),
                         TRef(po.ap[:, 64:128], p.buf), eng="scalar")
                    dma(dref(outdram[pr, t], oname), vs, eng="gpsimd")

    P.barrier()
    ar.reset(ov)
    ar16.reset(ov16)
    wob16 = ar16.alloc("wob16", DC * 2048)
    aoq16 = [ar16.alloc(f"aoq16{c}", 1024) for c in range(NPAIR)]
    bqslh = [ar16.alloc(f"bqh{i}", 512) for i in range(2)]
    bqsll = [ar16.alloc(f"bql{i}", 512) for i in range(2)]
    bkslh = [ar16.alloc(f"bkh{i}", 640) for i in range(2)]
    bksll = [ar16.alloc(f"bkl{i}", 640) for i in range(2)]
    bvsl = [ar.alloc(f"bv{i}", 5 * 130) for i in range(2)]
    bms = ar.alloc("bms", 5 * 512)
    aoq = [ar.alloc(f"aoq{c}", 512) for c in range(NPAIR)]
    besc = [[ar.alloc(f"besc{h}{i}", 512) for i in range(3)] for h in range(2)]
    bes = [[ar.alloc(f"bes{h}{i}", 512) for i in range(3)] for h in range(2)]
    brd = [ar.alloc(f"brd{h}", 512) for h in range(2)]
    bxr = [ar.alloc(f"bxr{i}", 1024) for i in range(2)]
    bx4 = [ar.alloc(f"bx4{i}", 1024) for i in range(2)]

    for c in range(DC):
        dma(TRef(wob16.ap[:, c * 2048:(c + 1) * 2048].rearrange(
            "p (hl j) -> p hl j", hl=2), wob16.buf),
            TRef(wobsp_d[c].rearrange("hl p j -> p hl j"), dbuf["wsp"]))

    libi_of = {}
    for li, (qi, k) in enumerate(LIVE_B):
        libi_of[(qi, k)] = li

    for qi, (qs, qw) in enumerate(QCS):
        kts = [k for (qj, k) in LIVE_B if qj == qi]
        k0 = kts[0]
        nk = len(kts)
        for ki, k in enumerate(kts):
            li = libi_of[(qi, k)]
            dma(TRef(bms.ap[:, ki * 512:ki * 512 + qw], bms.buf),
                TRef(bmask_d[li][:, 0:qw], Buf("bmd")))
        for pr in range(NPAIR):
            qslh = bqslh[pr % 2]
            qsll = bqsll[pr % 2]
            kslh = bkslh[pr % 2]
            ksll = bksll[pr % 2]
            vsl = bvsl[pr % 2]
            dma(TRef(qslh.ap[:, 0:qw], qslh.buf),
                dref(qbh_d[pr][:, qs:qs + qw], "qbh"))
            dma(TRef(qsll.ap[:, 0:qw], qsll.buf),
                dref(qbl_d[pr][:, qs:qs + qw], "qbl"))
            dma(TRef(kslh.ap[:, 0:nk * 128], kslh.buf),
                dref(kbh_d[pr][:, k0 * 128:(k0 + nk) * 128], "kbh"))
            dma(TRef(ksll.ap[:, 0:nk * 128], ksll.buf),
                dref(kbl_d[pr][:, k0 * 128:(k0 + nk) * 128], "kbl"))
            for ki in range(nk):
                dma(TRef(vsl.ap[:, ki * 130:(ki + 1) * 130], vsl.buf),
                    dref(vb_d[pr, k0 + ki], "vb"))
            pvp = [TRef(pb[6].ap[0:65, 0:qw], pb[6].buf),
                   TRef(pb[7].ap[0:65, 0:qw], pb[7].buf)]

            def emit_bpv(ki, _pvp=pvp, _nk=nk, _qw=qw, _vsl=vsl):
                for h in range(2):
                    e_t = TRef(bes[h][ki % 3].ap[:, 0:_qw], bes[h][ki % 3].buf)
                    mm(_pvp[h], TRef(_vsl.ap[:, ki * 130 + h * 65:
                                             ki * 130 + (h + 1) * 65], _vsl.buf),
                       e_t, start=(ki == 0), stop=(ki == _nk - 1), tag="bpv")

            pend = []
            for ki, k in enumerate(kts):
                bm = TRef(bms.ap[:, ki * 512:ki * 512 + qw], bms.buf)
                pos = [TRef(pb[3 * h + ki % 3].ap[:, 0:qw],
                            pb[3 * h + ki % 3].buf) for h in range(2)]
                qk_ops = [(kslh, qslh), (kslh, qsll), (ksll, qslh)]
                for pi, (ka, qa) in enumerate(qk_ops):
                    for h in range(2):
                        mm(pos[h],
                           TRef(ka.ap[h * 64:(h + 1) * 64,
                                      ki * 128:(ki + 1) * 128], ka.buf),
                           TRef(qa.ap[h * 64:(h + 1) * 64, 0:qw], qa.buf),
                           start=(pi == 0), stop=(pi == 2), tag="bqk")
                if len(pend) >= 2:
                    emit_bpv(pend.pop(0))
                for h in range(2):
                    hh = pr * 2 + h
                    sc = TRef(besc[h][ki % 3].ap[:, 0:qw], besc[h][ki % 3].buf)
                    stt(sc, bm, TRef(slope.ap[:, hh:hh + 1], slope.buf), pos[h],
                        ALU.mult, ALU.add, tag="bbias")
                    e_t = TRef(bes[h][ki % 3].ap[:, 0:qw], bes[h][ki % 3].buf)
                    act(e_t, sc, AF.Exp, tag="bexp")
                pend.append(ki)
            for p_ in pend:
                emit_bpv(p_)
            for h in range(2):
                rds = TRef(brd[h].ap[0:1, 0:qw], brd[h].buf)
                recip(rds, TRef(pvp[h].ap[64:65, :], pvp[h].buf))
                pbc = pb[h]
                bc = TRef(pbc.ap[0:64, 0:qw], pbc.buf)
                mm(bc, TRef(ones.ap[0:1, 0:64], ones.buf), rds,
                   start=True, stop=True, tag="bbc")
                bcs = TRef(bes[h][0].ap[0:64, 0:qw], bes[h][0].buf)
                copy(bcs, bc, eng="scalar")
                tt(TRef(aoq[pr].ap[h * 64:(h + 1) * 64, 0:qw], aoq[pr].buf),
                   TRef(pvp[h].ap[0:64, :], pvp[h].buf), bcs, ALU.mult)
            split(TRef(aoq16[pr].ap[:, 0:qw], aoq16[pr].buf),
                  TRef(aoq16[pr].ap[:, 512:512 + qw], aoq16[pr].buf),
                  TRef(aoq[pr].ap[:, 0:qw], aoq[pr].buf))
        for ti in range(qw // 128):
            t = qs // 128 + ti
            xr = bxr[ti % 2]
            dma(xr, dref(x3_d[t], "x3"))
            for dh in range(2):
                p = pb[2 * (ti % 2) + dh]
                po = TRef(p.ap, p.buf)
                for c in range(NPAIR):
                    mm3(po,
                        (TRef(aoq16[c].ap[:, ti * 128:(ti + 1) * 128],
                              aoq16[c].buf),
                         TRef(aoq16[c].ap[:, 512 + ti * 128:512 + (ti + 1) * 128],
                              aoq16[c].buf)),
                        (TRef(wob16.ap[:, c * 2048 + dh * 512:
                                       c * 2048 + (dh + 1) * 512], wob16.buf),
                         TRef(wob16.ap[:, c * 2048 + 1024 + dh * 512:
                                       c * 2048 + 1024 + (dh + 1) * 512],
                              wob16.buf)),
                        start=(c == 0), stop=(c == NPAIR - 1), tag="boproj")
                xs = bx4[ti % 2]
                tt(TRef(xs.ap[:, dh * 512:(dh + 1) * 512], xs.buf), po,
                   TRef(xr.ap[:, dh * 512:(dh + 1) * 512], xr.buf), ALU.add)
            dma(dref(x4_d[t], "x4"), bx4[ti % 2], eng="gpsimd")

    yout_tiles = [yout_d[t * 128:(t + 1) * 128, :] for t in range(NT_Q)]
    emit_ffn_sparse(x4_d, "x4", yout_tiles, "yout")

    P.materialize(stack)
    stack.close()
    return nc


_NC_CACHE = {}


def _get_nc():
    if "nc" not in _NC_CACHE:
        _NC_CACHE["nc"] = build_nc()
    return _NC_CACHE["nc"]


def _pack_proj_w(Wdm):
    return np.ascontiguousarray(
        Wdm.reshape(8, 128, 8, 128).transpose(2, 1, 0, 3).reshape(8, 128, 1024))


def _build_in_maps(L0_attn_a_x, L0_attn_a_wq, L0_attn_a_wk, L0_attn_a_wv,
                   L0_attn_a_wo, L0_attn_b_Wq, L0_attn_b_Wk, L0_attn_b_Wv,
                   L0_attn_b_Wo, L0_ffn_rw1, L0_ffn_rb1, L0_ffn_rw2, L0_ffn_rb2,
                   L0_ffn_ew1, L0_ffn_eb1, L0_ffn_ew2, L0_ffn_eb2):
    f32 = np.float32
    x = np.asarray(L0_attn_a_x, f32)
    wq = np.asarray(L0_attn_a_wq, f32)
    wk = np.asarray(L0_attn_a_wk, f32)
    wv = np.asarray(L0_attn_a_wv, f32)
    wo = np.asarray(L0_attn_a_wo, f32)
    Wbq = np.asarray(L0_attn_b_Wq, f32)
    Wbk = np.asarray(L0_attn_b_Wk, f32)
    Wbv = np.asarray(L0_attn_b_Wv, f32)
    Wbo = np.asarray(L0_attn_b_Wo, f32)
    rw1 = np.asarray(L0_ffn_rw1, f32)
    rb1 = np.asarray(L0_ffn_rb1, f32)
    rw2 = np.asarray(L0_ffn_rw2, f32)
    rb2 = np.asarray(L0_ffn_rb2, f32)
    ew1 = np.asarray(L0_ffn_ew1, f32)
    eb1 = np.asarray(L0_ffn_eb1, f32)
    ew2 = np.asarray(L0_ffn_ew2, f32)
    eb2 = np.asarray(L0_ffn_eb2, f32)

    wq_dm = wq.transpose(2, 0, 1).reshape(D, D)
    wk_dm = wk.transpose(2, 0, 1).reshape(D, D)
    wv_dm = wv.transpose(2, 0, 1).reshape(D, D)
    wq_dev = _pack_proj_w(wq_dm)
    wk_dev = _pack_proj_w(wk_dm)
    wv_dev = _pack_proj_w(wv_dm)
    woT = np.ascontiguousarray(wo.T)
    wo_dev = np.ascontiguousarray(woT.reshape(8, 128, 1024))
    bwq_dev = _pack_proj_w(np.ascontiguousarray(Wbq.T))
    bwk_dev = _pack_proj_w(np.ascontiguousarray(Wbk.T))
    bwv_dev = _pack_proj_w(np.ascontiguousarray(Wbv.T))
    bwo_dev = np.ascontiguousarray(Wbo.T.reshape(8, 128, 1024))
    rw1_dev = _pack_proj_w(rw1)
    rw2_dev = np.ascontiguousarray(rw2.reshape(8, 128, 8).transpose(1, 0, 2)
                                   .reshape(128, 64))
    rb1_dev = np.ascontiguousarray(rb1.reshape(8, 128).T)
    rb2_dev = np.ascontiguousarray(rb2.reshape(1, 8))
    eb1_dev = np.ascontiguousarray(eb1.reshape(8, 16, 128).transpose(2, 0, 1)
                                   .reshape(128, 128))
    eb1g_dev = np.ascontiguousarray(eb1.reshape(8, 16, 128).transpose(0, 2, 1))
    eb2_dev = np.ascontiguousarray(eb2)
    ew1_dev = np.ascontiguousarray(
        ew1.reshape(E, 8, 128, 16, 128).transpose(0, 2, 3, 1, 4)
        .reshape(E, 128, 16384))
    ew2_dev = np.ascontiguousarray(ew2.reshape(E, NHC, 128, 1024))
    hv = np.repeat(np.arange(16, dtype=f32)[None, :], 128, axis=0)

    shared = dict(wq=wq_dev, wk=wk_dev, wv=wv_dev, wo=wo_dev,
                  bwq=bwq_dev, bwk=bwk_dev, bwv=bwv_dev, bwo=bwo_dev,
                  rw1=rw1_dev, rw2=rw2_dev, rb1=rb1_dev, rb2=rb2_dev,
                  eb1=eb1_dev, eb1g=eb1g_dev, eb2=eb2_dev, ew1=ew1_dev,
                  ew2=ew2_dev)

    in_maps = []
    for core in range(8):
        b = core // 2
        parity = core % 2
        if parity == 0:
            xkv = np.concatenate([np.zeros((2048, D), f32), x[b, 0:2176]], 0)
        else:
            xkv = np.concatenate([x[b], np.zeros((128, D), f32)], 0)
        dl = attn_a_deltas(parity)
        deltas = np.ascontiguousarray(np.repeat(dl[None, :], 128, axis=0))
        bm = attn_b_bmask(parity)
        m = dict(shared)
        m["xkv"] = np.ascontiguousarray(xkv)
        m["deltas"] = deltas
        m["bmask"] = bm
        in_maps.append(m)
    return in_maps


def kernel(**inputs):
    in_maps = _build_in_maps(**inputs)
    nc = _get_nc()
    res = run_bass_kernel_spmd(nc, in_maps, core_ids=list(range(8)))
    out = np.zeros((B, S, D), np.float32)
    for core in range(8):
        b = core // 2
        parity = core % 2
        y = np.asarray(res.results[core]["y_out"])
        out[b, parity * 2048:(parity + 1) * 2048] = y[0:2048]
    return out


# version 21
# speedup vs baseline: 1.0327x; 1.0327x over previous
import numpy as np
from contextlib import ExitStack

import concourse.bass as bass
import concourse.mybir as mybir
from concourse.bass import IndirectOffsetOnAxis
from concourse.bass_utils import run_bass_kernel_spmd

F32 = mybir.dt.float32
AF = mybir.ActivationFunctionType
ALU = mybir.AluOpType
AX = mybir.AxisListType

D = 1024
DC = 8
NH = 16
DH = 64
NPAIR = 8
S = 4096
B = 4
TQ = 2176
NT_Q = 17
TKV = 4224
NT_KV = 33
QOFF = 2048
QCS = [(0, 512), (512, 512), (1024, 512), (1536, 512), (2048, 128)]
E = 8
HD = 2048
NHC = 16
NSLOT = 34
NS2 = NSLOT * 2
RTROWS = NSLOT * 256
YROWS = 5 * TQ
WIN = 128
ENGS = ("tensor", "vector", "scalar", "sync", "gpsimd")
DEBUG_STREAMS = False
STOP_AFTER_FFN1 = False
_LAST_PROG = None


def attn_a_live():
    out = []
    for qi, (qs, qw) in enumerate(QCS):
        kmax = min(NT_KV - 1, (QOFF + qs + qw - 1) // 128)
        for kt in range(kmax + 1):
            out.append((qi, kt))
    return out


def attn_a_deltas(parity):
    ds = []
    for (qi, kt) in attn_a_live():
        qs, qw = QCS[qi]
        d = 128 * kt - qs - QOFF
        if parity == 0 and kt < 16:
            d = 10000.0
        if parity == 1 and kt == 32:
            d = 10000.0
        ds.append(float(d))
    return np.array(ds, dtype=np.float32)


def attn_b_live():
    out = []
    for qi, (qs, qw) in enumerate(QCS):
        k0 = max(0, -(-(qs - 127) // 128))
        k1 = min(NT_Q - 1, (qs + qw + 126) // 128)
        for k in range(k0, k1 + 1):
            out.append((qi, k))
    return out


def attn_b_bmask(parity):
    tiles = []
    for (qi, k) in attn_b_live():
        qs, qw = QCS[qi]
        o = 128 * k - qs
        ki = np.arange(128)[:, None]
        q_in = np.arange(512)[None, :]
        rel = o + ki - q_in
        t = np.where((rel >= 0) & (rel < WIN), -rel.astype(np.float64), -1e32)
        if parity == 1 and k == 16 and qi != 4:
            t = np.full_like(t, -1e32)
        tiles.append(t.astype(np.float32))
    return np.stack(tiles)


LIVE_A = attn_a_live()
LIVE_B = attn_b_live()
N_LIVE_A = len(LIVE_A)
N_LIVE_B = len(LIVE_B)


class Buf:
    __slots__ = ("name", "writer", "readers")

    def __init__(self, name):
        self.name = name
        self.writer = None
        self.readers = []


N_DMA_SEMS = 72
N_HW_SEMS = 36


class Op:
    __slots__ = ("eng", "fn", "dma", "tag", "eidx", "dsem", "dval",
                 "need_inc", "waits", "iname", "rw")

    def __init__(self, eng, fn, dma, tag):
        self.eng = eng
        self.fn = fn
        self.dma = dma
        self.tag = tag
        self.eidx = -1
        self.dsem = -1
        self.dval = 0
        self.need_inc = False
        self.waits = []


class Prog:
    def __init__(self, nc):
        self.nc = nc
        self.ops = {e: [] for e in ENGS}
        self.dcount = [0] * N_DMA_SEMS
        self.bufsem = {}
        self.sem_next = 0
        self.sw_bufsem = {}
        self.sw_next = N_HW_SEMS

    def _sem_for(self, buf, eng):
        k = id(buf)
        if eng == "gpsimd":
            s = self.sw_bufsem.get(k)
            if s is None:
                if self.sw_next >= N_DMA_SEMS:
                    raise RuntimeError("swdge dma sem pool exhausted")
                s = self.sw_next
                self.sw_next += 1
                self.sw_bufsem[k] = s
            return s
        s = self.bufsem.get(k)
        if s is None:
            if self.sem_next >= N_HW_SEMS:
                raise RuntimeError("hwdge dma sem pool exhausted in one epoch")
            s = self.sem_next
            self.sem_next += 1
            self.bufsem[k] = s
        return s

    def emit(self, eng, fn, reads=(), writes=(), dma=False, tag=""):
        op = Op(eng, fn, dma, tag)
        op.rw = ([b.name for b in reads], [b.name for b in writes])
        op.eidx = len(self.ops[eng])
        self.ops[eng].append(op)
        deps = []
        for b in reads:
            if b.writer is not None:
                deps.append(b.writer)
        for b in writes:
            if b.writer is not None:
                deps.append(b.writer)
            deps.extend(b.readers)
        seen = set()
        for p in deps:
            if id(p) in seen or p is op:
                continue
            seen.add(id(p))
            if (p.eng == op.eng == "tensor") and not p.dma and not op.dma:
                continue
            if p.dma:
                op.waits.append(("dma", p.dsem, 16 * self.dcount[p.dsem]))
            else:
                op.waits.append(p)
                p.need_inc = True
        if dma:
            assert writes, "dma op must declare its destination buf first"
            s = self._sem_for(writes[0], eng)
            self.dcount[s] += 1
            op.dsem = s
            op.dval = 16 * self.dcount[s]
        for b in reads:
            b.readers.append(op)
        for b in writes:
            b.writer = op
            b.readers = []
        return op

    def barrier(self, bufs_to_reset=()):
        snapshot = {e: (self.ops[e][-1] if self.ops[e] else None) for e in ENGS}
        for p in snapshot.values():
            if p is not None and not p.dma:
                p.need_inc = True
        dwaits = [("dma", s, 16 * self.dcount[s])
                  for s in range(N_DMA_SEMS) if self.dcount[s]]
        for e in ENGS:
            op = Op(e, lambda eng: eng.nop(), False, "barrier")
            op.eidx = len(self.ops[e])
            for f, p in snapshot.items():
                if p is None:
                    continue
                if p.dma:
                    continue
                if f == e:
                    continue
                op.waits.append(p)
            op.waits.extend(dwaits)
            self.ops[e].append(op)
        self.bufsem = {}
        self.sem_next = 0
        self.sw_bufsem = {}
        self.sw_next = N_HW_SEMS
        for b in bufs_to_reset:
            b.writer = None
            b.readers = []

    def materialize(self, stack):
        nc = self.nc
        psem = {e: stack.enter_context(nc.semaphore(f"pg_{e}")) for e in ENGS}
        dsems = [stack.enter_context(nc.semaphore(f"dm{i}"))
                 for i in range(N_DMA_SEMS)]
        pcount = {}
        for e in ENGS:
            c = 0
            arr = []
            for op in self.ops[e]:
                if op.need_inc and not op.dma:
                    c += 1
                arr.append(c)
            pcount[e] = arr

        def run(e, eng):
            hi = {}
            for op in self.ops[e]:
                for p in op.waits:
                    if isinstance(p, tuple):
                        _, s, val = p
                        key = ("d", s)
                        sem = dsems[s]
                    else:
                        val = pcount[p.eng][p.eidx]
                        key = ("p", p.eng)
                        sem = psem[p.eng]
                    if hi.get(key, -1) >= val:
                        continue
                    hi[key] = val
                    eng.wait_ge(sem, val)
                inst = op.fn(eng)
                op.iname = getattr(inst, 'name', None)
                if op.dma:
                    inst.then_inc(dsems[op.dsem], 16)
                elif op.need_inc:
                    inst.then_inc(psem[e], 1)
            if e in ("gpsimd", "sync"):
                for s in range(N_DMA_SEMS):
                    if self.dcount[s]:
                        eng.wait_ge(dsems[s], 16 * self.dcount[s])

        with nc.Block() as blk:
            blk.tensor(lambda eng: run("tensor", eng))
            blk.vector(lambda eng: run("vector", eng))
            blk.scalar(lambda eng: run("scalar", eng))
            blk.sync(lambda eng: run("sync", eng))
            blk.gpsimd(lambda eng: run("gpsimd", eng))


class TRef:
    __slots__ = ("ap", "buf")

    def __init__(self, ap, buf):
        self.ap = ap
        self.buf = buf

    def __getitem__(self, idx):
        return TRef(self.ap[idx], self.buf)


class Arena:
    def __init__(self, nc, name, words, dtype=F32):
        self.h = nc.alloc_sbuf_tensor(name, [128, words], dtype)
        self.ap = self.h.ap()
        self.off = 0
        self.words = words

    def alloc(self, name, w, parts=128):
        w16 = (w + 7) & ~7
        assert self.off + w16 <= self.words, (
            f"SBUF arena overflow at {name}: {self.off}+{w16} > {self.words}")
        ap = self.ap[0:parts, self.off:self.off + w]
        self.off += w16
        return TRef(ap, Buf(name))

    def mark(self):
        return self.off

    def reset(self, off):
        self.off = off


def build_nc():
    nc = bass.Bass("TRN2", target_bir_lowering=False)
    f = nc.dram_tensor

    xkv_d = f("xkv", [TKV, D], F32, kind="ExternalInput").ap()
    wq_d = f("wq", [NPAIR, 128, 1024], F32, kind="ExternalInput").ap()
    wk_d = f("wk", [NPAIR, 128, 1024], F32, kind="ExternalInput").ap()
    wv_d = f("wv", [NPAIR, 128, 1024], F32, kind="ExternalInput").ap()
    wo_d = f("wo", [DC, 128, 1024], F32, kind="ExternalInput").ap()
    bwq_d = f("bwq", [NPAIR, 128, 1024], F32, kind="ExternalInput").ap()
    bwk_d = f("bwk", [NPAIR, 128, 1024], F32, kind="ExternalInput").ap()
    bwv_d = f("bwv", [NPAIR, 128, 1024], F32, kind="ExternalInput").ap()
    bwo_d = f("bwo", [DC, 128, 1024], F32, kind="ExternalInput").ap()
    rw1_d = f("rw1", [DC, 128, 1024], F32, kind="ExternalInput").ap()
    rw2_d = f("rw2", [128, 64], F32, kind="ExternalInput").ap()
    rb1_d = f("rb1", [128, 8], F32, kind="ExternalInput").ap()
    rb2_d = f("rb2", [1, 8], F32, kind="ExternalInput").ap()
    eb1_d = f("eb1", [128, 128], F32, kind="ExternalInput").ap()
    eb1g_d = f("eb1g", [E, 128, 16], F32, kind="ExternalInput").ap()
    eb2_d = f("eb2", [8, 1024], F32, kind="ExternalInput").ap()
    ew1_d = f("ew1", [E, 128, 16384], F32, kind="ExternalInput").ap()
    ew2_d = f("ew2", [E, NHC, 128, 1024], F32, kind="ExternalInput").ap()
    deltas_d = f("deltas", [128, N_LIVE_A], F32, kind="ExternalInput").ap()
    bmask_d = f("bmask", [N_LIVE_B, 128, 512], F32, kind="ExternalInput").ap()

    yout_d = f("y_out", [TQ, D], F32, kind="ExternalOutput").ap()

    _sk = "ExternalOutput" if DEBUG_STREAMS else "Internal"
    x2_d = f("x2s", [NT_Q, 128, D], F32, kind=_sk).ap()
    x3_d = f("x3s", [NT_Q, 128, D], F32, kind=_sk).ap()
    x4_d = f("x4s", [NT_Q, 128, D], F32, kind=_sk).ap()
    ao_d = f("aos", [NPAIR, 128, TQ], F32, kind=_sk).ap()
    BF16_ = mybir.dt.bfloat16
    qbh_d = f("qbhs", [NPAIR, 128, TQ], BF16_, kind="Internal").ap()
    qbl_d = f("qbls", [NPAIR, 128, TQ], BF16_, kind="Internal").ap()
    kbh_d = f("kbhs", [NPAIR, 128, TQ], BF16_, kind="Internal").ap()
    kbl_d = f("kbls", [NPAIR, 128, TQ], BF16_, kind="Internal").ap()
    vb_d = f("vbs", [NPAIR, NT_Q, 128, 130], F32, kind="Internal").ap()
    xhsp_d = f("xhsp", [NT_KV, 2, 128, 1024], BF16_, kind="Internal").ap()
    ew1sp_d = f("ew1sp", [NHC, E, 128, 2, 1024], BF16_, kind="Internal").ap()
    ew2sp_d = f("ew2sp", [NHC, E, 128, 2, 1024], BF16_, kind="Internal").ap()
    rw1sp_d = f("rw1sp", [DC, 2, 128, 1024], BF16_, kind="Internal").ap()
    wosp_d = f("wosp", [DC, 2, 128, 1024], BF16_, kind="Internal").ap()
    wobsp_d = f("wobsp", [DC, 2, 128, 1024], BF16_, kind="Internal").ap()
    xhtm_d = f("xhtm", [NT_Q, 128, 1024], F32, kind="Internal").ap()
    rt_d = f("rt", [RTROWS + 1, 8], F32, kind="Internal").ap()
    ysl_d = f("ysl", [YROWS, 1024], F32, kind="Internal").ap()

    dbg_d = None
    dbg_ctr = [0]
    if DEBUG_STREAMS:
        dbg_d = f("dbg", [64, 128, 512], F32, kind="ExternalOutput").ap()

    ident_np = np.eye(128, dtype=np.float32)
    mstrict_np = (np.arange(128)[:, None] < np.arange(128)[None, :]
                  ).astype(np.float32)
    u8_np = np.zeros((8, 8), np.float32)
    u8_np[np.triu_indices(8, 1)] = 1.0
    iota34_np = np.arange(NSLOT, dtype=np.float32).reshape(1, NSLOT)
    tokf_np = (np.arange(NT_Q)[None, :] * 128
               + np.arange(128)[:, None]).astype(np.float32)
    rtfill_np = np.zeros((128, NS2 * 8), np.float32)
    for s in range(NS2):
        rtfill_np[:, s * 8 + 2] = 4 * TQ + s
    dqk_np = (np.arange(512)[None, :] - np.arange(128)[:, None]).astype(np.float32)
    hvals_np = np.repeat(np.arange(16, dtype=np.float32)[None, :], 128, axis=0)
    ident_d = nc.inline_tensor(ident_np, name="identc").ap()
    dqk_d = nc.inline_tensor(dqk_np, name="dqkc").ap()
    hvals_d = nc.inline_tensor(np.ascontiguousarray(hvals_np), name="hvalsc").ap()
    mstrict_d = nc.inline_tensor(mstrict_np, name="mstrictc").ap()
    u8_d = nc.inline_tensor(u8_np, name="u8c").ap()
    iota34_d = nc.inline_tensor(iota34_np, name="iota34c").ap()
    tokf_d = nc.inline_tensor(tokf_np, name="tokfc").ap()
    rtfill_d = nc.inline_tensor(rtfill_np, name="rtfillc").ap()

    P = Prog(nc)
    global _LAST_PROG
    _LAST_PROG = P
    stack = ExitStack()

    pb = []
    for i in range(8):
        h = nc.alloc_psum_tensor(f"pb{i}", [128, 512], F32)
        pb.append(TRef(h.ap(), Buf(f"pb{i}")))

    BF16 = mybir.dt.bfloat16
    ar = Arena(nc, "arena", 28800)
    ar16 = Arena(nc, "arena16", 48400, BF16)

    ident = ar.alloc("ident", 128)
    ones = ar.alloc("ones", 128)
    hvals = ar.alloc("hvals", 16)
    slope = ar.alloc("slope", 16)
    slt1 = ar.alloc("slt1", 16)
    slt2 = ar.alloc("slt2", 16)
    constv = ar.alloc("constv", 8)
    _CONSTS = [1e-5, -1.0, 0.125]
    for i, v in enumerate(_CONSTS):
        nc.const_aps.register(F32, v, constv.ap[:, i:i + 1])
    persist_off = ar.mark()

    dbuf = {n: Buf(n) for n in ("xkv", "x2", "x3", "x4", "ao", "vb", "yout",
                                "qbh", "qbl", "kbh", "kbl", "wsp", "xhsp",
                                "xhtm", "rt", "ysl0", "ysl1", "ysl2", "ysl3",
                                "win")}

    def dma(dst, src, eng="sync", rb=None, wb=None, tag="dma"):
        reads = [src.buf] + (rb or [])
        writes = [dst.buf] + (wb or [])
        P.emit(eng, (lambda e, o=dst.ap, i=src.ap: e.dma_start(out=o, in_=i)),
               reads, writes, dma=True, tag=tag)

    def mm(out, lhsT, rhs, start, stop, tag="mm"):
        P.emit("tensor",
               (lambda e, o=out.ap, a=lhsT.ap, b=rhs.ap, s=start, t=stop:
                e.matmul(o, a, b, start=s, stop=t)),
               [lhsT.buf, rhs.buf], [out.buf], tag=tag)

    def trans(out, in_, tag="tr"):
        P.emit("tensor",
               (lambda e, o=out.ap, i=in_.ap, idn=ident.ap: e.transpose(o, i, idn)),
               [in_.buf, ident.buf], [out.buf], tag=tag)

    def act(out, in_, func, bias=0.0, scale=1.0, accum=None, eng="scalar", tag="act"):
        b_ap = bias.ap if isinstance(bias, TRef) else bias
        s_ap = scale.ap if isinstance(scale, TRef) else scale
        reads = [in_.buf]
        if isinstance(bias, TRef):
            reads.append(bias.buf)
        if isinstance(scale, TRef):
            reads.append(scale.buf)
        writes = [out.buf]
        if accum is not None:
            writes.append(accum.buf)
        a_ap = accum.ap if accum is not None else None
        P.emit(eng,
               (lambda e, o=out.ap, i=in_.ap, fn=func, bb=b_ap, ss=s_ap, aa=a_ap:
                e.activation(o, i, fn, bias=bb, scale=ss, accum_out=aa)),
               reads, writes, tag=tag)

    def ts(out, in0, s1, op0, s2=None, op1=None, eng="vector", tag="ts"):
        s1a = s1.ap if isinstance(s1, TRef) else s1
        s2a = s2.ap if isinstance(s2, TRef) else s2
        reads = [in0.buf]
        if isinstance(s1, TRef):
            reads.append(s1.buf)
        if isinstance(s2, TRef):
            reads.append(s2.buf)
        P.emit(eng,
               (lambda e, o=out.ap, i=in0.ap, a=s1a, b=s2a, o0=op0, o1=op1:
                e.tensor_scalar(o, i, a, b, o0) if o1 is None
                else e.tensor_scalar(o, i, a, b, o0, o1)),
               reads, [out.buf], tag=tag)

    def stt(out, in0, s, in1, op0, op1, eng="vector", tag="stt"):
        sa = s.ap if isinstance(s, TRef) else s
        reads = [in0.buf, in1.buf]
        if isinstance(s, TRef):
            reads.append(s.buf)
        P.emit(eng,
               (lambda e, o=out.ap, i0=in0.ap, ss=sa, i1=in1.ap, a=op0, b=op1:
                e.scalar_tensor_tensor(o, i0, ss, i1, a, b)),
               reads, [out.buf], tag=tag)

    def tt(out, a, b, op, eng="vector", tag="tt"):
        P.emit(eng,
               (lambda e, o=out.ap, x=a.ap, y=b.ap, oo=op: e.tensor_tensor(o, x, y, oo)),
               [a.buf, b.buf], [out.buf], tag=tag)

    def red(out, in_, op, negate=False, eng="vector", tag="red"):
        P.emit(eng,
               (lambda e, o=out.ap, i=in_.ap, oo=op, n=negate:
                e.tensor_reduce(o, i, AX.X, oo, negate=n)),
               [in_.buf], [out.buf], tag=tag)

    def recip(out, in_, eng="vector", tag="rcp"):
        P.emit(eng, (lambda e, o=out.ap, i=in_.ap: e.reciprocal(o, i)),
               [in_.buf], [out.buf], tag=tag)

    def copy(out, in_, eng="vector", scale=None, tag="cp"):
        if scale is not None:
            act(out, in_, AF.Copy, scale=scale, eng="scalar", tag=tag)
        elif eng == "scalar":
            P.emit(eng, (lambda e, o=out.ap, i=in_.ap: e.copy(o, i)),
                   [in_.buf], [out.buf], tag=tag)
        else:
            P.emit(eng, (lambda e, o=out.ap, i=in_.ap: e.tensor_copy(o, i)),
                   [in_.buf], [out.buf], tag=tag)

    def memset(dst, val, eng="vector", tag="ms"):
        P.emit(eng, (lambda e, o=dst.ap, v=val: e.memset(o, v)),
               [], [dst.buf], tag=tag)

    def dref(ap, name):
        return TRef(ap, dbuf[name])

    dbg_buf = Buf("dbg")

    def dbg_dump(src, parts=128, w=512):
        if dbg_d is None or dbg_ctr[0] >= 64:
            return
        i = dbg_ctr[0]
        dbg_ctr[0] += 1
        P.emit("gpsimd",
               (lambda e, o=dbg_d[i][0:parts, 0:w], s=src.ap:
                e.dma_start(out=o, in_=s)),
               [src.buf], [dbg_buf], dma=True, tag="dbg")

    def mm3(out, lh, rh, start, stop, tag="mm3"):
        combos = [(lh[0], rh[0]), (lh[0], rh[1]), (lh[1], rh[0])]
        for i, (a, b_) in enumerate(combos):
            mm(out, a, b_, start=(start and i == 0), stop=(stop and i == 2),
               tag=tag)

    def split(hi, lo, src, scale=None):
        if scale is None:
            copy(hi, src, eng="scalar")
            tt(lo, src, hi, ALU.subtract)
        else:
            act(hi, src, AF.Copy, scale=scale)
            stt(lo, src, scale, hi, ALU.mult, ALU.subtract)

    prep_f32 = [ar.alloc(f"prep{i}", 1024) for i in range(2)]
    prep_h = [ar16.alloc(f"preph{i}", 1024) for i in range(2)]
    prep_l = [ar16.alloc(f"prepl{i}", 1024) for i in range(2)]
    prep_units = []
    for c in range(DC):
        prep_units.append((wo_d[c], wosp_d[c][0], wosp_d[c][1]))
    for c in range(DC):
        prep_units.append((rw1_d[c], rw1sp_d[c][0], rw1sp_d[c][1]))
    for e in range(E):
        for ch in range(16):
            prep_units.append((ew1_d[e][:, ch * 1024:(ch + 1) * 1024],
                               ew1sp_d[ch, e][:, 0, :],
                               ew1sp_d[ch, e][:, 1, :]))
    for e in range(E):
        for hc in range(NHC):
            prep_units.append((ew2_d[e, hc], ew2sp_d[hc, e][:, 0, :],
                               ew2sp_d[hc, e][:, 1, :]))
    for c in range(DC):
        prep_units.append((bwo_d[c], wobsp_d[c][0], wobsp_d[c][1]))
    prep_pos = [0]

    def drain_prep(n):
        i0 = prep_pos[0]
        for i in range(i0, min(i0 + n, len(prep_units))):
            src_ap, dst_hi, dst_lo = prep_units[i]
            pin = prep_f32[i % 2]
            ph = prep_h[i % 2]
            pl = prep_l[i % 2]
            dma(pin, TRef(src_ap, Buf("prepsrc")), eng="gpsimd")
            copy(ph, pin, eng="gpsimd")
            tt(pl, pin, ph, ALU.subtract, eng="gpsimd")
            dma(TRef(dst_hi, dbuf["wsp"]), ph, eng="gpsimd")
            dma(TRef(dst_lo, dbuf["wsp"]), pl, eng="gpsimd")
        prep_pos[0] = min(i0 + n, len(prep_units))

    dma(ident, TRef(ident_d, Buf("identd")))
    dma(hvals, TRef(hvals_d, Buf("hvalsd")))
    memset(TRef(ones.ap[0:1, :], ones.buf), 1.0)

    a15 = float(np.float32(1.0 / 15.0))
    b15 = float(np.float32(1.0 / 15.0 - a15))
    ln2 = float(np.log(2.0))
    ln2hi = float(np.float32(ln2))
    ln2lo = float(np.float32(ln2 - ln2hi))
    ts(slt1, hvals, a15, ALU.mult)
    ts(slt2, hvals, b15, ALU.mult)
    tt(slt1, slt1, slt2, ALU.add)
    ts(slt1, slt1, -8.0, ALU.mult)
    ts(slt2, slt1, ln2lo, ALU.mult)
    ts(slt1, slt1, ln2hi, ALU.mult)
    tt(slt1, slt1, slt2, ALU.add)
    act(slope, slt1, AF.Exp)
    for i, v in enumerate(_CONSTS):
        memset(TRef(constv.ap[:, i:i + 1], constv.buf), v)
    P.barrier()

    def emit_ln(xin, xh, stats, scratch):
        ss = TRef(stats.ap[:, 0:1], stats.buf)
        sm = TRef(stats.ap[:, 1:2], stats.buf)
        mean = TRef(stats.ap[:, 2:3], stats.buf)
        ssn = TRef(stats.ap[:, 3:4], stats.buf)
        nv = TRef(stats.ap[:, 4:5], stats.buf)
        sd = TRef(stats.ap[:, 5:6], stats.buf)
        rstd = TRef(stats.ap[:, 6:7], stats.buf)
        act(scratch, xin, AF.Square, accum=ss)
        red(sm, xin, ALU.add)
        ts(mean, sm, 1.0 / D, ALU.mult)
        ts(ssn, ss, 1.0 / D, ALU.mult)
        ts(nv, mean, mean, ALU.mult, ssn, ALU.subtract)
        act(sd, nv, AF.Sqrt, bias=1e-5, scale=-1.0)
        recip(rstd, sd)
        ts(xh, xin, mean, ALU.subtract, rstd, ALU.mult)

    ov = ar.mark()
    ov16 = ar16.mark()
    dqk = ar.alloc("dqk", 512)
    deltas = ar.alloc("deltas", N_LIVE_A)
    dma(dqk, TRef(dqk_d, Buf("dqkd")))
    dma(deltas, TRef(deltas_d, Buf("deltasd")))
    vtm = ar.alloc("vtm", NT_KV * 130)
    ov1 = ar.mark()
    ov16_1 = ar16.mark()
    xin2 = [ar.alloc(f"xin{i}", 1024) for i in range(2)]
    xh2 = [ar.alloc(f"xh{i}", 1024) for i in range(2)]
    stat2 = [ar.alloc(f"st{i}", 8) for i in range(2)]
    scr2 = [ar.alloc(f"scr{i}", 1024) for i in range(2)]
    stg = [[ar16.alloc(f"stg{hl}{i}", 1024) for i in range(2)] for hl in range(2)]

    for t in range(NT_KV):
        xi = xin2[t % 2]
        xh = xh2[t % 2]
        dma(xi, dref(xkv_d[t * 128:(t + 1) * 128, :], "xkv"))
        emit_ln(xi, xh, stat2[t % 2], scr2[t % 2])
        for half in range(2):
            p = pb[(2 * t + half) % 4]
            for cc in range(4):
                c = half * 4 + cc
                trans(TRef(p.ap[:, cc * 128:(cc + 1) * 128], p.buf),
                      TRef(xh.ap[:, c * 128:(c + 1) * 128], xh.buf))
            split(TRef(stg[0][t % 2].ap[:, half * 512:(half + 1) * 512],
                       stg[0][t % 2].buf),
                  TRef(stg[1][t % 2].ap[:, half * 512:(half + 1) * 512],
                       stg[1][t % 2].buf),
                  TRef(p.ap[:, 0:512], p.buf))
        for hl in range(2):
            dma(TRef(xhsp_d[t, hl], dbuf["xhsp"]),
                TRef(stg[hl][t % 2].ap, stg[hl][t % 2].buf), eng="gpsimd")
        drain_prep(2)

    P.barrier()
    ar.reset(ov1)
    ar16.reset(ov16_1)
    wsl = [ar.alloc("wsl0", 3 * 1024)]
    es = [[ar.alloc(f"es{h}{i}", 512) for i in range(3)] for h in range(2)]
    rd_sb = [ar.alloc(f"rd{h}", 512) for h in range(2)]
    ast = [ar.alloc(f"ast{i}", 512) for i in range(2)]
    kt2h = ar16.alloc("kt2h", TKV)
    kt2l = ar16.alloc("kt2l", TKV)
    qt2h = ar16.alloc("qt2h", TQ)
    qt2l = ar16.alloc("qt2l", TQ)
    w16 = ar16.alloc("w16", 6 * 1024)
    xstr = [ar16.alloc(f"xstr{i}", DC * 2 * 512) for i in range(3)]
    KV_CHUNKS = [(i * 512, 512) for i in range(8)] + [(4096, 128)]

    def _wslice(base, c):
        return (TRef(w16.ap[:, base * 1024 + c * 128:base * 1024 + (c + 1) * 128],
                     w16.buf),
                TRef(w16.ap[:, (base + 3) * 1024 + c * 128:
                            (base + 3) * 1024 + (c + 1) * 128], w16.buf))

    li_of = {}
    for li, (qi, kt) in enumerate(LIVE_A):
        li_of[(qi, kt)] = li

    def emit_chunk_loads(ci):
        cs_, cw_ = KV_CHUNKS[ci]
        xs_ = xstr[ci % 3]
        for ti in range(cw_ // 128):
            for hl in range(2):
                dma(TRef(xs_.ap[:, (ti * 2 + hl) * 1024:
                                (ti * 2 + hl + 1) * 1024], xs_.buf),
                    TRef(xhsp_d[cs_ // 128 + ti, hl], dbuf["xhsp"]),
                    eng="sync" if hl == 0 else "scalar")

    def emit_wload_loads(pr):
        w = wsl[0]
        dma(TRef(w.ap[:, 0:1024], w.buf), TRef(wq_d[pr], Buf("wqd")))
        dma(TRef(w.ap[:, 1024:2048], w.buf), TRef(wk_d[pr], Buf("wkd")))
        dma(TRef(w.ap[:, 2048:3072], w.buf), TRef(wv_d[pr], Buf("wvd")))

    def emit_wsplit():
        w = wsl[0]
        for b in range(3):
            wp = TRef(w.ap[:, b * 1024:(b + 1) * 1024], w.buf)
            hi = TRef(w16.ap[:, b * 1024:(b + 1) * 1024], w16.buf)
            lo = TRef(w16.ap[:, (b + 3) * 1024:(b + 4) * 1024], w16.buf)
            copy(hi, wp, eng="gpsimd")
            tt(lo, wp, hi, ALU.subtract, eng="gpsimd")

    def emit_wload(pr):
        emit_wload_loads(pr)
        emit_wsplit()

    for pr in range(NPAIR):
        if pr == 0:
            emit_wload(0)
            emit_chunk_loads(0)
            emit_chunk_loads(1)
        memset(vtm, 1.0)
        for ci, (cs, cw) in enumerate(KV_CHUNKS):
            if ci + 2 < len(KV_CHUNKS):
                emit_chunk_loads(ci + 2)
            elif pr + 1 < NPAIR:
                emit_chunk_loads(ci + 2 - len(KV_CHUNKS))
            if ci == 6 and pr + 1 < NPAIR:
                emit_wload_loads(pr + 1)
            xs = xstr[ci % 3]
            ntile = cw // 128
            xsv = xs.ap[:, 0:ntile * 2048].rearrange(
                "p (ti hl cj) -> p ti hl cj", hl=2, cj=1024)

            def _x(c, hl, j0, jw, _xsv=xsv, _xs=xs):
                if jw == 128 and j0 % 128 == 0:
                    ti = j0 // 128
                    return TRef(_xsv[:, ti, hl, c * 128:(c + 1) * 128], _xs.buf)
                assert j0 == 0
                return TRef(_xsv[:, :, hl, c * 128:(c + 1) * 128], _xs.buf)

            po = TRef(pb[4 + ci % 2].ap[:, 0:cw], pb[4 + ci % 2].buf)
            for c in range(DC):
                mm3(po, _wslice(1, c), (_x(c, 0, 0, cw), _x(c, 1, 0, cw)),
                    start=(c == 0), stop=(c == DC - 1), tag="kproj")
            split(TRef(kt2h.ap[:, cs:cs + cw], kt2h.buf),
                  TRef(kt2l.ap[:, cs:cs + cw], kt2l.buf), po)
            for ti in range(cw // 128):
                t = cs // 128 + ti
                po2 = TRef(pb[6 + t % 2].ap[:, 0:128], pb[6 + t % 2].buf)
                for c in range(DC):
                    xh_ = _x(c, 0, ti * 128, 128)
                    xl_ = _x(c, 1, ti * 128, 128)
                    wh_, wl_ = _wslice(2, c)
                    for i, (a, b_) in enumerate([(xh_, wh_), (xh_, wl_),
                                                 (xl_, wh_)]):
                        mm(po2, a, b_, start=(c == 0 and i == 0),
                           stop=(c == DC - 1 and i == 2), tag="vproj")
                copy(TRef(vtm.ap[:, t * 130:t * 130 + 64], vtm.buf),
                     TRef(pb[6 + t % 2].ap[:, 0:64], pb[6 + t % 2].buf))
                copy(TRef(vtm.ap[:, t * 130 + 65:t * 130 + 129], vtm.buf),
                     TRef(pb[6 + t % 2].ap[:, 64:128], pb[6 + t % 2].buf),
                     eng="scalar")
            drain_prep(2)
            if ci >= 4:
                qi = ci - 4
                qs, qw = QCS[qi]
                po3 = TRef(pb[2 + ci % 2].ap[:, 0:qw], pb[2 + ci % 2].buf)
                for c in range(DC):
                    mm3(po3, _wslice(0, c), (_x(c, 0, 0, qw), _x(c, 1, 0, qw)),
                        start=(c == 0), stop=(c == DC - 1), tag="qproj")
                split(TRef(qt2h.ap[:, qs:qs + qw], qt2h.buf),
                      TRef(qt2l.ap[:, qs:qs + qw], qt2l.buf), po3, scale=0.125)
        if pr + 1 < NPAIR:
            emit_wsplit()
        for qi, (qs, qw) in enumerate(QCS):
            kts = [kt for (qj, kt) in LIVE_A if qj == qi]
            nk = len(kts)
            pvp = [TRef(pb[6].ap[0:65, 0:qw], pb[6].buf),
                   TRef(pb[7].ap[0:65, 0:qw], pb[7].buf)]

            def emit_pv(ki, kt, _pvp=pvp, _nk=nk, _qw=qw):
                for h in range(2):
                    e_t = TRef(es[h][ki % 3].ap[:, 0:_qw], es[h][ki % 3].buf)
                    mm(_pvp[h], TRef(vtm.ap[:, kt * 130 + h * 65:
                                            kt * 130 + (h + 1) * 65], vtm.buf),
                       e_t, start=(ki == 0), stop=(ki == _nk - 1), tag="pv")

            pend = []
            for ki, kt in enumerate(kts):
                li = li_of[(qi, kt)]
                dl = TRef(deltas.ap[:, li:li + 1], deltas.buf)
                d_univ = 128 * kt - qs - QOFF
                openall = (d_univ <= -127) and (16 <= kt <= 31)
                pos = [TRef(pb[3 * h + ki % 3].ap[:, 0:qw],
                            pb[3 * h + ki % 3].buf) for h in range(2)]
                qk_ops = [(kt2h, qt2h), (kt2h, qt2l), (kt2l, qt2h)]
                for pi, (ka, qb_) in enumerate(qk_ops):
                    for h in range(2):
                        mm(pos[h],
                           TRef(ka.ap[h * 64:(h + 1) * 64,
                                      kt * 128:(kt + 1) * 128], ka.buf),
                           TRef(qb_.ap[h * 64:(h + 1) * 64, qs:qs + qw], qb_.buf),
                           start=(pi == 0), stop=(pi == 2), tag="qk")
                if len(pend) >= 2:
                    emit_pv(*pend.pop(0))
                for h in range(2):
                    e_t = TRef(es[h][ki % 3].ap[:, 0:qw], es[h][ki % 3].buf)
                    act(e_t, pos[h], AF.Exp, tag="exp")
                    if not openall:
                        stt(e_t, TRef(dqk.ap[:, 0:qw], dqk.buf), dl, e_t,
                            ALU.is_ge, ALU.mult, tag="zmask")
                pend.append((ki, kt))
            for p_ in pend:
                emit_pv(*p_)
            for h in range(2):
                rds = TRef(rd_sb[h].ap[0:1, 0:qw], rd_sb[h].buf)
                recip(rds, TRef(pvp[h].ap[64:65, :], pvp[h].buf))
                pbc = pb[h]
                bc = TRef(pbc.ap[0:64, 0:qw], pbc.buf)
                mm(bc, TRef(ones.ap[0:1, 0:64], ones.buf), rds,
                   start=True, stop=True, tag="bcast")
                bcs = TRef(es[h][0].ap[0:64, 0:qw], es[h][0].buf)
                copy(bcs, bc, eng="scalar")
                a_s = ast[qi % 2]
                tt(TRef(a_s.ap[h * 64:(h + 1) * 64, 0:qw], a_s.buf),
                   TRef(pvp[h].ap[0:64, :], pvp[h].buf), bcs, ALU.mult)
            a_s = ast[qi % 2]
            dma(dref(ao_d[pr][:, qs:qs + qw], "ao"),
                TRef(a_s.ap[:, 0:qw], a_s.buf), eng="gpsimd")
        drain_prep(10)


    P.barrier()
    ar.reset(ov)
    ar16.reset(ov16)
    drain_prep(40)
    wo16 = ar16.alloc("wo16", DC * 2048)
    aoin = [ar.alloc(f"aoin{i}", 1024) for i in range(2)]
    aoi16 = [ar16.alloc(f"aoi16{i}", 2048) for i in range(2)]
    xres = [ar.alloc(f"xres{i}", 1024) for i in range(2)]
    x2st = [ar.alloc(f"x2st{i}", 1024) for i in range(2)]
    for c in range(DC):
        dma(TRef(wo16.ap[:, c * 2048:(c + 1) * 2048].rearrange(
            "p (hl j) -> p hl j", hl=2), wo16.buf),
            TRef(wosp_d[c].rearrange("hl p j -> p hl j"), dbuf["wsp"]))
    for t in range(NT_Q):
        ai = aoin[t % 2]
        a16 = aoi16[t % 2]
        xr = xres[t % 2]
        for c in range(NPAIR):
            dma(TRef(ai.ap[:, c * 128:(c + 1) * 128], ai.buf),
                dref(ao_d[c][:, t * 128:(t + 1) * 128], "ao"))
        dma(xr, dref(xkv_d[QOFF + t * 128:QOFF + (t + 1) * 128, :], "xkv"))
        split(TRef(a16.ap[:, 0:1024], a16.buf),
              TRef(a16.ap[:, 1024:2048], a16.buf), ai)
        for dh in range(2):
            p = pb[2 * (t % 2) + dh]
            po = TRef(p.ap, p.buf)
            for c in range(DC):
                mm3(po,
                    (TRef(a16.ap[:, c * 128:(c + 1) * 128], a16.buf),
                     TRef(a16.ap[:, 1024 + c * 128:1024 + (c + 1) * 128],
                          a16.buf)),
                    (TRef(wo16.ap[:, c * 2048 + dh * 512:
                                  c * 2048 + (dh + 1) * 512], wo16.buf),
                     TRef(wo16.ap[:, c * 2048 + 1024 + dh * 512:
                                  c * 2048 + 1024 + (dh + 1) * 512], wo16.buf)),
                    start=(c == 0), stop=(c == DC - 1), tag="oproj")
            xs = x2st[t % 2]
            tt(TRef(xs.ap[:, dh * 512:(dh + 1) * 512], xs.buf), po,
               TRef(xr.ap[:, dh * 512:(dh + 1) * 512], xr.buf), ALU.add)
        dma(dref(x2_d[t], "x2"), x2st[t % 2], eng="gpsimd")

    def emit_ffn(instream, inname, outstream, outname, dbg=False):
        P.barrier()
        ar.reset(ov)
        ar16.reset(ov16)
        drain_prep(1000)
        xf16 = [ar16.alloc(f"fxf{hl}", 4 * 1024) for hl in range(2)]
        eh16 = [ar16.alloc(f"feh{i}", 1024) for i in range(NHC)]
        ew1sl = [ar16.alloc(f"few1{i}", 2048) for i in range(2)]
        ew2sl = [ar16.alloc(f"few2{i}", 2048) for i in range(2)]
        rw116 = [ar16.alloc(f"frw1{i}", 2048) for i in range(2)]
        r1T = [ar.alloc(f"fr1T{c}", 512) for c in range(DC)]
        eh32 = [ar.alloc(f"feh32{i}", 512) for i in range(2)]
        ytile = [ar.alloc(f"fy{i}", 1024) for i in range(4)]
        fxin = [ar.alloc(f"fxin{i}", 1024) for i in range(2)]
        fxh = [ar.alloc(f"fxh{i}", 1024) for i in range(2)]
        fst = [ar.alloc(f"fst{i}", 8) for i in range(2)]
        fscr2 = [ar.alloc(f"fscr{i}", 1024) for i in range(2)]
        lt = [ar.alloc(f"flt{i}", 8) for i in range(4)]
        wsb = [ar.alloc(f"fw{i}", 8) for i in range(4)]
        wts = [ar.alloc(f"fwts{i}", 24) for i in range(4)]
        wTs = [ar.alloc(f"fwT{i}", 128) for i in range(4)]
        ost = [ar.alloc(f"fo{i}", 1024) for i in range(2)]
        rb1p = ar.alloc("rb1p", 8)
        rb2r = ar.alloc("rb2r", 8)
        eb1p = ar.alloc("eb1p", 128)
        eb2s = ar.alloc("eb2s", 1024)
        rw2s = ar.alloc("rw2s", 64)
        dma(rb1p, TRef(rb1_d, Buf("rb1d")))
        dma(TRef(rb2r.ap[0:1, :], rb2r.buf), TRef(rb2_d, Buf("rb2d")))
        dma(eb1p, TRef(eb1_d, Buf("eb1d")))
        dma(TRef(eb2s.ap[0:8, :], eb2s.buf), TRef(eb2_d, Buf("eb2d")))
        dma(rw2s, TRef(rw2_d, Buf("rw2d")))

        def xpair(c, nt):
            outs = []
            for hl in range(2):
                v = xf16[hl].ap[:, 0:nt * 1024].rearrange(
                    "p (ti cj) -> p ti cj", cj=1024)
                outs.append(TRef(v[:, :, c * 128:(c + 1) * 128], xf16[hl].buf))
            return tuple(outs)

        def xpair_t(c, ti):
            return (TRef(xf16[0].ap[:, ti * 1024 + c * 128:
                                    ti * 1024 + (c + 1) * 128], xf16[0].buf),
                    TRef(xf16[1].ap[:, ti * 1024 + c * 128:
                                    ti * 1024 + (c + 1) * 128], xf16[1].buf))

        for qi, (qs, qw) in enumerate(QCS):
            nt = qw // 128
            for ti in range(nt):
                t = qs // 128 + ti
                xi = fxin[ti % 2]
                xh = fxh[ti % 2]
                dma(xi, dref(instream[t], inname))
                emit_ln(xi, xh, fst[ti % 2], fscr2[ti % 2])
                tt(ytile[ti], xi, xh, ALU.add)
                for half in range(2):
                    p = pb[(2 * ti + half) % 4]
                    for cc in range(4):
                        c = half * 4 + cc
                        trans(TRef(p.ap[:, cc * 128:(cc + 1) * 128], p.buf),
                              TRef(xh.ap[:, c * 128:(c + 1) * 128], xh.buf))
                    split(TRef(xf16[0].ap[:, ti * 1024 + half * 512:
                                          ti * 1024 + (half + 1) * 512],
                               xf16[0].buf),
                          TRef(xf16[1].ap[:, ti * 1024 + half * 512:
                                          ti * 1024 + (half + 1) * 512],
                               xf16[1].buf),
                          TRef(p.ap[:, 0:512], p.buf))
            for oc in range(DC):
                rw = rw116[oc % 2]
                dma(TRef(rw.ap.rearrange("p (hl j) -> p hl j", hl=2), rw.buf),
                    TRef(rw1sp_d[oc].rearrange("hl p j -> p hl j"),
                         dbuf["wsp"]))
                p = pb[2 + oc % 2]
                po = TRef(p.ap[:, 0:qw], p.buf)
                for c in range(DC):
                    mm3(po,
                        (TRef(rw.ap[:, c * 128:(c + 1) * 128], rw.buf),
                         TRef(rw.ap[:, 1024 + c * 128:1024 + (c + 1) * 128],
                              rw.buf)),
                        xpair(c, nt),
                        start=(c == 0), stop=(c == DC - 1), tag="r1")
                act(TRef(r1T[oc].ap[:, 0:qw], r1T[oc].buf), po, AF.Silu,
                    bias=TRef(rb1p.ap[:, oc:oc + 1], rb1p.buf))
            if dbg and qi < 2:
                dbg_dump(TRef(fxh[0].ap[:, 0:512], fxh[0].buf), w=512)
                dbg_dump(TRef(r1T[0].ap[:, 0:qw], r1T[0].buf), w=qw)
            for ti in range(nt):
                p = pb[4]
                po = TRef(p.ap[:, 0:8], p.buf)
                for oc in range(DC):
                    mm(po, TRef(r1T[oc].ap[:, ti * 128:(ti + 1) * 128],
                                r1T[oc].buf),
                       TRef(rw2s.ap[:, oc * 8:(oc + 1) * 8], rw2s.buf),
                       start=(oc == 0), stop=False)
                mm(po, TRef(ones.ap[0:1, 0:128], ones.buf),
                   TRef(rb2r.ap[0:1, 0:8], rb2r.buf), start=False, stop=True)
                l = lt[ti]
                copy(l, po)
                st = wts[ti]
                m1 = TRef(st.ap[:, 0:1], st.buf)
                nm1 = TRef(st.ap[:, 1:2], st.buf)
                m2 = TRef(st.ap[:, 2:3], st.buf)
                ds_ = TRef(st.ap[:, 3:4], st.buf)
                dr_ = TRef(st.ap[:, 4:5], st.buf)
                eqt = TRef(st.ap[:, 8:16], st.buf)
                l2t = TRef(st.ap[:, 16:24], st.buf)
                red(m1, l, ALU.max)
                ts(nm1, m1, -1.0, ALU.mult)
                ts(eqt, l, m1, ALU.is_ge)
                stt(l2t, eqt, -1e30, l, ALU.mult, ALU.add)
                red(m2, l2t, ALU.max)
                ts(l2t, l, m2, ALU.is_ge)
                act(eqt, l, AF.Exp, bias=nm1)
                tt(eqt, eqt, l2t, ALU.mult)
                red(ds_, eqt, ALU.add)
                recip(dr_, ds_)
                ts(wsb[ti], eqt, dr_, ALU.mult)
                p2 = pb[5]
                pw = TRef(p2.ap[0:8, 0:128], p2.buf)
                trans(pw, wsb[ti])
                copy(TRef(wTs[ti].ap[0:8, :], wTs[ti].buf), pw)
            if dbg and qi < 2:
                dbg_dump(lt[0], w=8)
                dbg_dump(wsb[0], w=8)
            for ei in range(E):
                for hc in range(NHC):
                    w1 = ew1sl[hc % 2]
                    dma(TRef(w1.ap.rearrange("p (hl j) -> p hl j", hl=2),
                             w1.buf),
                        TRef(ew1sp_d[ei][:, :, hc * 1024:(hc + 1) * 1024]
                             .rearrange("hl p j -> p hl j"), dbuf["wsp"]),
                        tag="ew1")
                    p = pb[2 + hc % 2]
                    po = TRef(p.ap[:, 0:qw], p.buf)
                    for c in range(DC):
                        mm3(po,
                            (TRef(w1.ap[:, c * 128:(c + 1) * 128], w1.buf),
                             TRef(w1.ap[:, 1024 + c * 128:1024 + (c + 1) * 128],
                                  w1.buf)),
                            xpair(c, nt),
                            start=(c == 0), stop=(c == DC - 1), tag="e1")
                    e32 = eh32[hc % 2]
                    act(TRef(e32.ap[:, 0:qw], e32.buf), po, AF.Silu,
                        bias=TRef(eb1p.ap[:, ei * 16 + hc:ei * 16 + hc + 1],
                                  eb1p.buf), tag="silu1")
                    split(TRef(eh16[hc].ap[:, 0:qw], eh16[hc].buf),
                          TRef(eh16[hc].ap[:, 512:512 + qw], eh16[hc].buf),
                          TRef(e32.ap[:, 0:qw], e32.buf))
                for hc in range(NHC):
                    e2w = ew2sl[hc % 2]
                    dma(TRef(e2w.ap.rearrange("p (hl j) -> p hl j", hl=2),
                             e2w.buf),
                        TRef(ew2sp_d[ei, hc].rearrange("hl p j -> p hl j"),
                             dbuf["wsp"]), tag="ew2")
                    for ti in range(nt):
                        ehp = (TRef(eh16[hc].ap[:, ti * 128:(ti + 1) * 128],
                                    eh16[hc].buf),
                               TRef(eh16[hc].ap[:, 512 + ti * 128:
                                                512 + (ti + 1) * 128],
                                    eh16[hc].buf))
                        for dh in range(2):
                            p = pb[ti * 2 + dh]
                            po = TRef(p.ap, p.buf)
                            mm3(po, ehp,
                                (TRef(e2w.ap[:, dh * 512:(dh + 1) * 512],
                                      e2w.buf),
                                 TRef(e2w.ap[:, 1024 + dh * 512:
                                             1024 + (dh + 1) * 512], e2w.buf)),
                                start=(hc == 0), stop=(hc == NHC - 1), tag="e2")
                order = [(1, 0), (1, 1), (0, 0), (0, 1)] if nt > 1 else [(0, 0), (0, 1)]
                order += [(ti, dh) for ti in range(2, nt) for dh in range(2)]
                for (ti, dh) in order:
                    p = pb[ti * 2 + dh]
                    stt(TRef(ytile[ti].ap[:, dh * 512:(dh + 1) * 512],
                             ytile[ti].buf),
                        TRef(p.ap, p.buf),
                        TRef(wsb[ti].ap[:, ei:ei + 1], wsb[ti].buf),
                        TRef(ytile[ti].ap[:, dh * 512:(dh + 1) * 512],
                             ytile[ti].buf),
                        ALU.mult, ALU.add, tag="e2evac")
                if dbg and qi < 2 and ei == 0:
                    dbg_dump(TRef(eh32[0].ap[:, 0:qw], eh32[0].buf), w=qw)
                    dbg_dump(TRef(ytile[0].ap[:, 0:512], ytile[0].buf), w=512)
            for ti in range(nt):
                t = qs // 128 + ti
                o = ost[ti % 2]
                for dh in range(2):
                    p = pb[4 + dh]
                    po = TRef(p.ap, p.buf)
                    mm(po, TRef(wTs[ti].ap[0:8, :], wTs[ti].buf),
                       TRef(eb2s.ap[0:8, dh * 512:(dh + 1) * 512], eb2s.buf),
                       start=True, stop=True, tag="eb2")
                    tt(TRef(o.ap[:, dh * 512:(dh + 1) * 512], o.buf),
                       TRef(ytile[ti].ap[:, dh * 512:(dh + 1) * 512], ytile[ti].buf),
                       po, ALU.add)
                dma(dref(outstream[t], outname), o, eng="gpsimd")
            if dbg and qi < 2:
                dbg_dump(TRef(ytile[0].ap[:, 0:512], ytile[0].buf), w=512)


    posI_t = [nc.alloc_sbuf_tensor(f"posI{i}", [128, 8], mybir.dt.int32).ap()
              for i in range(2)]
    destI4_t = [nc.alloc_sbuf_tensor(f"destI{i}", [128, 1],
                                     mybir.dt.int32).ap() for i in range(6)]
    tokI_t = [nc.alloc_sbuf_tensor(f"tokI{i}", [128, 1], mybir.dt.int32).ap()
              for i in range(4)]
    wIdxA_t = nc.alloc_sbuf_tensor("wIdxA", [128, NSLOT], mybir.dt.int32).ap()
    wIdx16_t = [nc.alloc_sbuf_tensor(f"wIdx16_{i}", [128, 16],
                                     mybir.dt.int32).ap() for i in range(2)]
    segI_t = nc.alloc_sbuf_tensor("segI", [1, 8], mybir.dt.int32).ap()
    posI = [TRef(a, Buf(f"posI{i}")) for i, a in enumerate(posI_t)]
    destI4 = [TRef(a, Buf(f"destI{i}")) for i, a in enumerate(destI4_t)]
    tokI = [TRef(a, Buf(f"tokI{i}")) for i, a in enumerate(tokI_t)]
    wIdxA = TRef(wIdxA_t, Buf("wIdxA"))
    wIdx16 = [TRef(a, Buf(f"wIdx16_{i}")) for i, a in enumerate(wIdx16_t)]
    segI = TRef(segI_t, Buf("segI"))

    def idma_gather(out, in_, idx, tag="ig"):
        P.emit("gpsimd",
               (lambda e, o=out.ap, i=in_.ap, x=idx.ap:
                e.indirect_dma_start(out=o, out_offset=None, in_=i,
                                     in_offset=IndirectOffsetOnAxis(x, 0))),
               [in_.buf, idx.buf], [out.buf], dma=True, tag=tag)

    def idma_scatter(out, in_, idx, tag="is"):
        P.emit("gpsimd",
               (lambda e, o=out.ap, i=in_.ap, x=idx.ap:
                e.indirect_dma_start(out=o, out_offset=IndirectOffsetOnAxis(x, 0),
                                     in_=i, in_offset=None)),
               [in_.buf, idx.buf], [out.buf], dma=True, tag=tag)

    def emit_ffn_sparse(instream, inname, outstream, outname):
        P.barrier()
        ar.reset(ov)
        ar16.reset(ov16)
        drain_prep(1000)
        xf16 = [ar16.alloc(f"sxf{hl}", 4 * 1024) for hl in range(2)]
        gx16 = [ar16.alloc(f"sgx{hl}", 2048) for hl in range(2)]
        eh16 = [ar16.alloc(f"seh{i}", 512) for i in range(NHC)]
        ew1sl = [ar16.alloc(f"sew1{i}", 2048) for i in range(3)]
        ew2sl = [ar16.alloc(f"sew2{i}", 2048) for i in range(3)]
        rw116 = [ar16.alloc(f"srw1{i}", 2048) for i in range(2)]
        fxin = [ar.alloc(f"sfxin{i}", 1024) for i in range(2)]
        fxh = [ar.alloc(f"sfxh{i}", 1024) for i in range(2)]
        fst = [ar.alloc(f"sfst{i}", 8) for i in range(2)]
        fscr2 = [ar.alloc(f"sfscr{i}", 1024) for i in range(2)]
        wS = ar.alloc("swS", NT_Q * 8)
        rankS = ar.alloc("srankS", NT_Q * 8)
        jS = ar.alloc("sjS", NT_Q * 8)
        wTsS = ar.alloc("swTsS", TQ)
        baseR = ar.alloc("sbaseR", 8)
        segrow = ar.alloc("ssegrow", 8)
        eslB = ar.alloc("seslB", NSLOT)
        tokfS = ar.alloc("stokf", NT_Q)
        iot34 = ar.alloc("siot34", NSLOT)
        segtmp = ar.alloc("ssegtmp", 80)
        msb = ar.alloc("smsb", 128)
        u8sb = ar.alloc("su8", 8)
        onec = ar.alloc("sonec", 8)
        rb1p = ar.alloc("rb1p", 8)
        rb2r = ar.alloc("rb2r", 8)
        eb1p = ar.alloc("eb1p", 128)
        eb2s = ar.alloc("eb2s", 1024)
        rw2s = ar.alloc("rw2s", 64)
        rfill = ar.alloc("srfill", NS2 * 8)
        hcrow = ar.alloc("shcrow", 16)
        ovS = ar.mark()
        r1T = [ar.alloc(f"sr1T{c}", 512) for c in range(DC)]
        lt = [ar.alloc(f"slt{i}", 8) for i in range(4)]
        wts = [ar.alloc(f"swts{i}", 56) for i in range(4)]
        pay = [ar.alloc(f"spay{i}", 64) for i in range(2)]
        posf = ar.alloc("sposf", 8)
        destf = ar.alloc("sdestf", 8)
        tokrep = ar.alloc("stokrep", 8)
        ind3S = ar.alloc("sind3S", NT_Q * 8)
        ind3T = [ar.alloc(f"sind3T{i}", 128) for i in range(2)]
        dma(rb1p, TRef(rb1_d, Buf("rb1d")))
        dma(TRef(rb2r.ap[0:1, :], rb2r.buf), TRef(rb2_d, Buf("rb2d")))
        dma(eb1p, TRef(eb1_d, Buf("eb1d")))
        dma(TRef(eb2s.ap[0:8, :], eb2s.buf), TRef(eb2_d, Buf("eb2d")))
        dma(rw2s, TRef(rw2_d, Buf("rw2d")))
        dma(TRef(msb.ap, msb.buf), TRef(mstrict_d, Buf("msd")))
        dma(TRef(u8sb.ap[0:8, :], u8sb.buf), TRef(u8_d, Buf("u8d")))
        dma(TRef(iot34.ap[0:1, :], iot34.buf), TRef(iota34_d, Buf("i34d")))
        dma(tokfS, TRef(tokf_d, Buf("tokfd")))
        dma(rfill, TRef(rtfill_d, Buf("rtfd")))
        ts(TRef(hcrow.ap, hcrow.buf), TRef(tokfS.ap[:, 0:16], tokfS.buf),
           TRef(tokfS.ap[:, 0:1], tokfS.buf), ALU.subtract, 8.0, ALU.mult)
        memset(TRef(baseR.ap[0:1, :], baseR.buf), 0.0)
        memset(onec, 1.0)
        dma(TRef(rt_d[0:RTROWS].rearrange("(s p) k -> p s k", p=128),
                 dbuf["rt"]),
            TRef(rfill.ap.rearrange("p (s k) -> p s k", k=8), rfill.buf),
            eng="gpsimd")

        def xpair(c, nt):
            outs = []
            for hl in range(2):
                v = xf16[hl].ap[:, 0:nt * 1024].rearrange(
                    "p (ti cj) -> p ti cj", cj=1024)
                outs.append(TRef(v[:, :, c * 128:(c + 1) * 128], xf16[hl].buf))
            return tuple(outs)

        for qi, (qs, qw) in enumerate(QCS):
            nt = qw // 128
            for ti in range(nt):
                t = qs // 128 + ti
                xi = fxin[ti % 2]
                xh = fxh[ti % 2]
                dma(xi, dref(instream[t], inname))
                emit_ln(xi, xh, fst[ti % 2], fscr2[ti % 2])
                dma(TRef(xhtm_d[t], dbuf["xhtm"]), xh, eng="gpsimd")
                for half in range(2):
                    p = pb[(2 * ti + half) % 4]
                    for cc in range(4):
                        c = half * 4 + cc
                        trans(TRef(p.ap[:, cc * 128:(cc + 1) * 128], p.buf),
                              TRef(xh.ap[:, c * 128:(c + 1) * 128], xh.buf))
                    split(TRef(xf16[0].ap[:, ti * 1024 + half * 512:
                                          ti * 1024 + (half + 1) * 512],
                               xf16[0].buf),
                          TRef(xf16[1].ap[:, ti * 1024 + half * 512:
                                          ti * 1024 + (half + 1) * 512],
                               xf16[1].buf),
                          TRef(p.ap[:, 0:512], p.buf))
            for oc in range(DC):
                rw = rw116[oc % 2]
                dma(TRef(rw.ap.rearrange("p (hl j) -> p hl j", hl=2), rw.buf),
                    TRef(rw1sp_d[oc].rearrange("hl p j -> p hl j"),
                         dbuf["wsp"]))
                p = pb[4 + oc % 2]
                po = TRef(p.ap[:, 0:qw], p.buf)
                for c in range(DC):
                    mm3(po,
                        (TRef(rw.ap[:, c * 128:(c + 1) * 128], rw.buf),
                         TRef(rw.ap[:, 1024 + c * 128:1024 + (c + 1) * 128],
                              rw.buf)),
                        xpair(c, nt),
                        start=(c == 0), stop=(c == DC - 1), tag="r1")
                act(TRef(r1T[oc].ap[:, 0:qw], r1T[oc].buf), po, AF.Silu,
                    bias=TRef(rb1p.ap[:, oc:oc + 1], rb1p.buf))
            for ti in range(nt):
                t = qs // 128 + ti
                p = pb[6]
                po = TRef(p.ap[:, 0:8], p.buf)
                for oc in range(DC):
                    mm(po, TRef(r1T[oc].ap[:, ti * 128:(ti + 1) * 128],
                                r1T[oc].buf),
                       TRef(rw2s.ap[:, oc * 8:(oc + 1) * 8], rw2s.buf),
                       start=(oc == 0), stop=False)
                mm(po, TRef(ones.ap[0:1, 0:128], ones.buf),
                   TRef(rb2r.ap[0:1, 0:8], rb2r.buf), start=False, stop=True)
                l = lt[ti]
                copy(l, po)
                st = wts[ti]
                m1 = TRef(st.ap[:, 0:1], st.buf)
                nm1 = TRef(st.ap[:, 1:2], st.buf)
                m2 = TRef(st.ap[:, 2:3], st.buf)
                ds_ = TRef(st.ap[:, 3:4], st.buf)
                dr_ = TRef(st.ap[:, 4:5], st.buf)
                m3 = TRef(st.ap[:, 5:6], st.buf)
                eq1 = TRef(st.ap[:, 8:16], st.buf)
                l2t = TRef(st.ap[:, 16:24], st.buf)
                eq2 = TRef(st.ap[:, 24:32], st.buf)
                l3t = TRef(st.ap[:, 32:40], st.buf)
                emt = TRef(st.ap[:, 40:48], st.buf)
                red(m1, l, ALU.max)
                ts(nm1, m1, -1.0, ALU.mult)
                ts(eq1, l, m1, ALU.is_ge)
                stt(l2t, eq1, -1e30, l, ALU.mult, ALU.add)
                red(m2, l2t, ALU.max)
                ts(eq2, l2t, m2, ALU.is_ge)
                stt(l3t, eq2, -1e30, l2t, ALU.mult, ALU.add)
                red(m3, l3t, ALU.max)
                ind3 = TRef(ind3S.ap[:, t * 8:(t + 1) * 8], ind3S.buf)
                ts(ind3, l, m3, ALU.is_ge)
                ts(eq2, l, m2, ALU.is_ge)
                act(emt, l, AF.Exp, bias=nm1)
                tt(emt, emt, eq2, ALU.mult)
                red(ds_, emt, ALU.add)
                recip(dr_, ds_)
                wsl_ = TRef(wS.ap[:, t * 8:(t + 1) * 8], wS.buf)
                ts(wsl_, emt, dr_, ALU.mult)
                pw = TRef(pb[7].ap[0:8, 0:128], pb[7].buf)
                trans(pw, wsl_)
                copy(TRef(wTsS.ap[0:8, t * 128:(t + 1) * 128], wTsS.buf), pw)
                pr_ = TRef(pb[7].ap[:, 128:136], pb[7].buf)
                mm(pr_, TRef(msb.ap, msb.buf), ind3, start=True, stop=False,
                   tag="rank")
                mm(pr_, TRef(ones.ap[0:1, 0:128], ones.buf),
                   TRef(baseR.ap[0:1, 0:8], baseR.buf), start=False, stop=True,
                   tag="rankb")
                copy(TRef(rankS.ap[:, t * 8:(t + 1) * 8], rankS.buf), pr_)
                pc_ = TRef(pb[7].ap[0:1, 136:144], pb[7].buf)
                mm(pc_, TRef(onec.ap[:, 0:1], onec.buf), ind3,
                   start=True, stop=True, tag="cnt")
                tt(TRef(baseR.ap[0:1, 0:8], baseR.buf),
                   TRef(baseR.ap[0:1, 0:8], baseR.buf), pc_, ALU.add)
                pi_ = TRef(pb[6].ap[0:8, 0:128], pb[6].buf)
                trans(pi_, ind3)
                it_ = ind3T[ti % 2]
                copy(TRef(it_.ap[0:8, :], it_.buf), pi_)
                pj_ = TRef(pb[6].ap[:, 256:264], pb[6].buf)
                mm(pj_, TRef(it_.ap[0:8, :], it_.buf),
                   TRef(u8sb.ap[0:8, 0:8], u8sb.buf), start=True, stop=True,
                   tag="jmm")
                copy(TRef(jS.ap[:, t * 8:(t + 1) * 8], jS.buf), pj_)
        a_ = TRef(segtmp.ap[0:1, 0:8], segtmp.buf)
        ts(a_, TRef(baseR.ap[0:1, 0:8], baseR.buf), 255.0, ALU.add)
        f_ = TRef(segtmp.ap[0:1, 8:16], segtmp.buf)
        fi_ = TRef(segtmp.ap[0:1, 56:64], segtmp.buf)
        g_ = TRef(segtmp.ap[0:1, 64:72], segtmp.buf)
        ts(f_, a_, 1.0 / 256.0, ALU.mult)
        copy(TRef(segI_t, segI.buf), f_)
        copy(fi_, TRef(segI_t, segI.buf))
        tt(g_, fi_, f_, ALU.is_gt)
        tt(fi_, fi_, g_, ALU.subtract)
        ts(a_, fi_, 256.0, ALU.mult)
        r2_ = TRef(segrow.ap[0:1, 0:8], segrow.buf)
        pp_ = TRef(pb[0].ap[0:8, 0:1], pb[0].buf)
        mm(pp_, a_, TRef(ones.ap[0:1, 0:1], ones.buf),
           start=True, stop=True, tag="segT")
        pf_ = TRef(segtmp.ap[0:8, 16:17], segtmp.buf)
        copy(pf_, pp_)
        ps_ = TRef(pb[0].ap[0:1, 8:16], pb[0].buf)
        mm(ps_, pf_, TRef(u8sb.ap[0:8, 0:8], u8sb.buf), start=True, stop=True,
           tag="seg")
        copy(r2_, ps_)
        st_ = TRef(segtmp.ap[0:1, 24:32], segtmp.buf)
        ts(st_, r2_, 1.0 / 256.0, ALU.mult)
        es1 = TRef(segtmp.ap[0:1, 32:33], segtmp.buf)
        memset(es1, -1.0)
        esacc = TRef(segtmp.ap[0:1, 40:40 + NSLOT], segtmp.buf)
        memset(esacc, -1.0)
        for e in range(E):
            stt(esacc, TRef(iot34.ap[0:1, :], iot34.buf),
                TRef(st_.ap[0:1, e:e + 1], segtmp.buf), esacc,
                ALU.is_ge, ALU.add)
        pe_ = TRef(pb[0].ap[:, 16:16 + NSLOT], pb[0].buf)
        mm(pe_, TRef(ones.ap[0:1, 0:128], ones.buf), esacc,
           start=True, stop=True, tag="eslb")
        copy(eslB, pe_)
        pg_ = TRef(pb[1].ap[:, 0:8], pb[1].buf)
        mm(pg_, TRef(ones.ap[0:1, 0:128], ones.buf), r2_,
           start=True, stop=True, tag="segb")
        segB = TRef(segtmp.ap[:, 48:56], segtmp.buf)
        copy(segB, pg_)
        for py_ in pay:
            memset(TRef(py_.ap, py_.buf), 0.0)
        for t in range(NT_Q):
            pf = TRef(posf.ap, posf.buf)
            tt(pf, TRef(rankS.ap[:, t * 8:(t + 1) * 8], rankS.buf), segB,
               ALU.add)
            ts(pf, pf, float(RTROWS), ALU.subtract)
            tt(pf, pf, TRef(ind3S.ap[:, t * 8:(t + 1) * 8], ind3S.buf),
               ALU.mult)
            ts(pf, pf, float(RTROWS), ALU.add)
            pI = posI[t % 2]
            copy(TRef(posI_t[t % 2], pI.buf), pf)
            jd = TRef(destf.ap, destf.buf)
            ts(jd, TRef(jS.ap[:, t * 8:(t + 1) * 8], jS.buf), 4.0, ALU.min)
            ts(jd, jd, float(TQ), ALU.mult,
               TRef(tokfS.ap[:, t:t + 1], tokfS.buf), ALU.add)
            tr_ = TRef(tokrep.ap, tokrep.buf)
            ts(tr_, TRef(wS.ap[:, t * 8:(t + 1) * 8], wS.buf), 0.0,
               ALU.mult, TRef(tokfS.ap[:, t:t + 1], tokfS.buf), ALU.add)
            py = pay[t % 2]
            copy(TRef(py.ap[:, 0:64:8], py.buf), tr_)
            copy(TRef(py.ap[:, 1:64:8], py.buf),
                 TRef(wS.ap[:, t * 8:(t + 1) * 8], wS.buf))
            copy(TRef(py.ap[:, 2:64:8], py.buf), jd)
            for e in range(E):
                idma_scatter(TRef(rt_d, dbuf["rt"]),
                             TRef(py.ap[:, e * 8:(e + 1) * 8], py.buf),
                             TRef(posI_t[t % 2][:, e:e + 1], pI.buf), tag="rtsc")
        P.barrier()
        ar.reset(ovS)
        eh32 = [ar.alloc(f"seh32{i}", 256) for i in range(2)]
        ebcols = [ar.alloc(f"sebc{i}", 16) for i in range(2)]
        rts = [ar.alloc(f"srts{i}", 8) for i in range(4)]
        gxh = [ar.alloc(f"sgxh{i}", 1024) for i in range(4)]
        eo_sb = [ar.alloc(f"seo{i}", 1024) for i in range(4)]
        acc = [ar.alloc(f"sacc{i}", 1024) for i in range(2)]
        ldb = [ar.alloc(f"sldb{i}", 1024) for i in range(2)]
        ost = [ar.alloc(f"sost{i}", 1024) for i in range(2)]
        wiAll = ar.alloc("swiA", NSLOT)
        wi16All = ar.alloc("swi16A", NSLOT * 16)
        ts(TRef(wiAll.ap, wiAll.buf), TRef(eslB.ap, eslB.buf), 128.0,
           ALU.mult, TRef(tokfS.ap[:, 0:1], tokfS.buf), ALU.add)
        copy(TRef(wIdxA_t, wIdxA.buf), TRef(wiAll.ap, wiAll.buf))
        for s in range(NSLOT):
            ts(TRef(wi16All.ap[:, s * 16:(s + 1) * 16], wi16All.buf),
               TRef(hcrow.ap, hcrow.buf),
               TRef(wiAll.ap[:, s:s + 1], wiAll.buf), ALU.add)

        def slot_wconv(s):
            copy(TRef(wIdx16_t[s % 2], wIdx16[s % 2].buf),
                 TRef(wi16All.ap[:, s * 16:(s + 1) * 16], wi16All.buf))

        def slot_rt(s):
            for sub in range(2):
                b = (2 * s + sub) % 4
                b6 = (2 * s + sub) % 6
                rts_ = rts[b]
                dma(rts_, TRef(rt_d[s * 256 + sub * 128:
                                    s * 256 + sub * 128 + 128, :], dbuf["rt"]))
                copy(TRef(tokI_t[b], tokI[b].buf),
                     TRef(rts_.ap[:, 0:1], rts_.buf))
                copy(TRef(destI4_t[b6], destI4[b6].buf),
                     TRef(rts_.ap[:, 2:3], rts_.buf))

        def slot_gx(s):
            for sub in range(2):
                b = (2 * s + sub) % 4
                idma_gather(gxh[b], TRef(xhtm_d.rearrange("t p j -> (t p) j"),
                                         dbuf["xhtm"]), tokI[b], tag="gxh")
            idma_gather(ebcols[s % 2],
                        TRef(eb1g_d.rearrange("e p h -> (e p) h"),
                             Buf("eb1gd")),
                        TRef(wIdxA_t[:, s:s + 1], wIdxA.buf), tag="geb1")

        def slot_scatter(s):
            for sub in range(2):
                b = (2 * s + sub) % 4
                b6 = (2 * s + sub) % 6
                idma_scatter(TRef(ysl_d, dbuf[f"ysl{s % 4}"]), eo_sb[b],
                             destI4[b6], tag="eosc")

        def gxp(c):
            outs = []
            for hl in range(2):
                v = gx16[hl].ap.rearrange("p (ti cj) -> p ti cj", cj=1024)
                outs.append(TRef(v[:, :, c * 128:(c + 1) * 128],
                                 gx16[hl].buf))
            return tuple(outs)

        slot_rt(0)
        slot_wconv(0)
        slot_gx(0)
        for s in range(NSLOT):
            if s + 1 < NSLOT:
                slot_rt(s + 1)
                slot_wconv(s + 1)
            for sub in range(2):
                b = (2 * s + sub) % 4
                for half in range(2):
                    p = pb[half]
                    for cc in range(4):
                        c = half * 4 + cc
                        trans(TRef(p.ap[:, cc * 128:(cc + 1) * 128], p.buf),
                              TRef(gxh[b].ap[:, c * 128:(c + 1) * 128],
                                   gxh[b].buf))
                    split(TRef(gx16[0].ap[:, sub * 1024 + half * 512:
                                          sub * 1024 + (half + 1) * 512],
                               gx16[0].buf),
                          TRef(gx16[1].ap[:, sub * 1024 + half * 512:
                                          sub * 1024 + (half + 1) * 512],
                               gx16[1].buf),
                          TRef(p.ap[:, 0:512], p.buf))
            if s + 1 < NSLOT:
                slot_gx(s + 1)
            def gath_w1(hc):
                w1 = ew1sl[(s * 16 + hc) % 3]
                idma_gather(w1,
                            TRef(ew1sp_d.rearrange(
                                "c e p hl j -> (c e p) hl j"),
                                 dbuf["wsp"]),
                            TRef(wIdx16_t[s % 2][:, hc:hc + 1],
                                 wIdx16[s % 2].buf), tag="gw1")

            def gath_w2(hc):
                e2w = ew2sl[(s * 16 + hc) % 3]
                idma_gather(e2w,
                            TRef(ew2sp_d.rearrange(
                                "c e p hl j -> (c e p) hl j"), dbuf["wsp"]),
                            TRef(wIdx16_t[s % 2][:, hc:hc + 1],
                                 wIdx16[s % 2].buf), tag="gw2")

            gath_w1(0)
            gath_w1(1)
            for hc in range(NHC):
                if hc + 2 < NHC:
                    gath_w1(hc + 2)
                w1 = ew1sl[(s * 16 + hc) % 3]
                p = pb[2 + hc % 2]
                po = TRef(p.ap[:, 0:256], p.buf)
                for c in range(DC):
                    mm3(po,
                        (TRef(w1.ap[:, c * 128:(c + 1) * 128], w1.buf),
                         TRef(w1.ap[:, 1024 + c * 128:1024 + (c + 1) * 128],
                              w1.buf)),
                        gxp(c), start=(c == 0), stop=(c == DC - 1), tag="se1")
                e32 = eh32[hc % 2]
                act(TRef(e32.ap, e32.buf), po, AF.Silu,
                    bias=TRef(ebcols[s % 2].ap[:, hc:hc + 1],
                              ebcols[s % 2].buf), tag="silu1x")
                split(TRef(eh16[hc].ap[:, 0:256], eh16[hc].buf),
                      TRef(eh16[hc].ap[:, 256:512], eh16[hc].buf),
                      TRef(e32.ap, e32.buf))
                if hc >= NHC - 2:
                    gath_w2(hc - (NHC - 2))
            for hc in range(NHC):
                if hc + 2 < NHC:
                    gath_w2(hc + 2)
                e2w = ew2sl[(s * 16 + hc) % 3]
                for sub in range(2):
                    ehp = (TRef(eh16[hc].ap[:, sub * 128:(sub + 1) * 128],
                                eh16[hc].buf),
                           TRef(eh16[hc].ap[:, 256 + sub * 128:
                                            256 + (sub + 1) * 128],
                                eh16[hc].buf))
                    for dh in range(2):
                        p = pb[4 + sub * 2 + dh]
                        po = TRef(p.ap, p.buf)
                        mm3(po, ehp,
                            (TRef(e2w.ap[:, dh * 512:(dh + 1) * 512], e2w.buf),
                             TRef(e2w.ap[:, 1024 + dh * 512:
                                         1024 + (dh + 1) * 512], e2w.buf)),
                            start=(hc == 0), stop=(hc == NHC - 1), tag="se2")
            for sub in range(2):
                b = (2 * s + sub) % 4
                eo = eo_sb[b]
                for dh in range(2):
                    p = pb[4 + sub * 2 + dh]
                    ts(TRef(eo.ap[:, dh * 512:(dh + 1) * 512], eo.buf),
                       TRef(p.ap, p.buf),
                       TRef(rts[b].ap[:, 1:2], rts[b].buf), ALU.mult)
            if s >= 1:
                slot_scatter(s - 1)
        slot_scatter(NSLOT - 1)
        yslbufs = [dbuf[f"ysl{i}"] for i in range(4)]
        ldb4 = [ldb[0], ldb[1], fxh[0]]
        for t in range(NT_Q):
            a = acc[t % 2]
            xi = fxin[t % 2]
            dma(xi, dref(instream[t], inname))
            for j in range(3):
                P.emit("sync",
                       (lambda e, o=ldb4[j].ap, i=ysl_d[j * TQ + t * 128:
                                                        j * TQ + (t + 1) * 128, :]:
                        e.dma_start(out=o, in_=i)),
                       [b_ for b_ in yslbufs], [ldb4[j].buf], dma=True, tag="yld")
            dma(eo_sb[t % 2], TRef(xhtm_d[t], dbuf["xhtm"]))
            tt(a, xi, eo_sb[t % 2], ALU.add)
            for j in range(3):
                tt(a, a, ldb4[j], ALU.add)
            o = ost[t % 2]
            for dh in range(2):
                po = TRef(pb[2 + 2 * (t % 2) + dh].ap,
                          pb[2 + 2 * (t % 2) + dh].buf)
                mm(po, TRef(wTsS.ap[0:8, t * 128:(t + 1) * 128], wTsS.buf),
                   TRef(eb2s.ap[0:8, dh * 512:(dh + 1) * 512], eb2s.buf),
                   start=True, stop=True, tag="eb2")
                tt(TRef(o.ap[:, dh * 512:(dh + 1) * 512], o.buf),
                   TRef(a.ap[:, dh * 512:(dh + 1) * 512], a.buf),
                   po, ALU.add)
            dma(dref(outstream[t], outname), o, eng="gpsimd")


    emit_ffn_sparse(x2_d, "x2", x3_d, "x3")

    if STOP_AFTER_FFN1:
        P.barrier()
        P.materialize(stack)
        stack.close()
        return nc

    P.barrier()
    ar.reset(ov)
    ar16.reset(ov16)
    xb16 = [ar16.alloc(f"bxf{hl}", NT_Q * 1024) for hl in range(2)]
    bwsl = [ar.alloc(f"bws{i}", 1024) for i in range(2)]
    bw16 = ar16.alloc("bw16", 2048)
    bxin = [ar.alloc(f"bxi{i}", 1024) for i in range(2)]
    bxh = [ar.alloc(f"bxh{i}", 1024) for i in range(2)]
    bst = [ar.alloc(f"bst{i}", 8) for i in range(2)]
    bscr2 = [ar.alloc(f"bscr{i}", 1024) for i in range(2)]
    bsq = [ar.alloc(f"bsq{i}", 128) for i in range(4)]
    bqn = [ar.alloc(f"bqn{i}", 128) for i in range(4)]
    bfmh = [ar16.alloc(f"bfmh{i}", 128) for i in range(4)]
    bfml = [ar16.alloc(f"bfml{i}", 128) for i in range(4)]
    bvs = [ar.alloc(f"bvs{i}", 130) for i in range(4)]
    bnrm = [ar.alloc(f"bnr{i}", 8) for i in range(4)]

    def x3pair(c, t):
        return (TRef(xb16[0].ap[:, t * 1024 + c * 128:t * 1024 + (c + 1) * 128],
                     xb16[0].buf),
                TRef(xb16[1].ap[:, t * 1024 + c * 128:t * 1024 + (c + 1) * 128],
                     xb16[1].buf))

    for t in range(NT_Q):
        xi = bxin[t % 2]
        xh = bxh[t % 2]
        dma(xi, dref(x3_d[t], "x3"))
        emit_ln(xi, xh, bst[t % 2], bscr2[t % 2])
        for half in range(2):
            p = pb[(2 * t + half) % 4]
            for cc in range(4):
                c = half * 4 + cc
                trans(TRef(p.ap[:, cc * 128:(cc + 1) * 128], p.buf),
                      TRef(xh.ap[:, c * 128:(c + 1) * 128], xh.buf))
            split(TRef(xb16[0].ap[:, t * 1024 + half * 512:
                                  t * 1024 + (half + 1) * 512], xb16[0].buf),
                  TRef(xb16[1].ap[:, t * 1024 + half * 512:
                                  t * 1024 + (half + 1) * 512], xb16[1].buf),
                  TRef(p.ap[:, 0:512], p.buf))

    for _b in bvs:
        memset(TRef(_b.ap[:, 64:65], _b.buf), 1.0)
        memset(TRef(_b.ap[:, 129:130], _b.buf), 1.0)

    for widx, (wdev, outdram, oname, do_norm) in enumerate(
            [(bwq_d, (qbh_d, qbl_d), ("qbh", "qbl"), True),
             (bwk_d, (kbh_d, kbl_d), ("kbh", "kbl"), True),
             (bwv_d, vb_d, "vb", False)]):
        for pr in range(NPAIR):
            wb = bwsl[pr % 2]
            dma(wb, TRef(wdev[pr], Buf("bwd")))
            split(TRef(bw16.ap[:, 0:1024], bw16.buf),
                  TRef(bw16.ap[:, 1024:2048], bw16.buf), wb)

            def bfinish(t, _pr=pr, _out=outdram, _on=oname):
                qn = bqn[t % 4]
                p2 = pb[5 + t % 3]
                pt = TRef(p2.ap[:, 0:128], p2.buf)
                trans(pt, qn)
                fmh = bfmh[t % 4]
                fml = bfml[t % 4]
                copy(fmh, pt, eng="scalar")
                tt(fml, pt, fmh, ALU.subtract)
                dma(dref(_out[0][_pr][:, t * 128:(t + 1) * 128], _on[0]),
                    fmh, eng="gpsimd")
                dma(dref(_out[1][_pr][:, t * 128:(t + 1) * 128], _on[1]),
                    fml, eng="gpsimd")

            for t in range(NT_Q):
                p = pb[2 + t % 3]
                po = TRef(p.ap[:, 0:128], p.buf)
                for c in range(DC):
                    xh_, xl_ = x3pair(c, t)
                    wh_ = TRef(bw16.ap[:, c * 128:(c + 1) * 128], bw16.buf)
                    wl_ = TRef(bw16.ap[:, 1024 + c * 128:1024 + (c + 1) * 128],
                               bw16.buf)
                    for i, (a, b_) in enumerate([(xh_, wh_), (xh_, wl_),
                                                 (xl_, wh_)]):
                        mm(po, a, b_, start=(c == 0 and i == 0),
                           stop=(c == DC - 1 and i == 2), tag="bproj")
                if do_norm:
                    st = bnrm[t % 4]
                    ss0 = TRef(st.ap[:, 0:1], st.buf)
                    ss1 = TRef(st.ap[:, 1:2], st.buf)
                    r0 = TRef(st.ap[:, 2:3], st.buf)
                    r1 = TRef(st.ap[:, 3:4], st.buf)
                    sq = bsq[t % 4]
                    act(TRef(sq.ap[:, 0:64], sq.buf),
                        TRef(po.ap[:, 0:64], p.buf), AF.Square, accum=ss0)
                    act(TRef(sq.ap[:, 64:128], sq.buf),
                        TRef(po.ap[:, 64:128], p.buf), AF.Square, accum=ss1)
                    act(r0, ss0, AF.Sqrt)
                    act(r1, ss1, AF.Sqrt)
                    ts(r0, r0, 1e-6, ALU.max)
                    ts(r1, r1, 1e-6, ALU.max)
                    recip(r0, r0)
                    recip(r1, r1)
                    qn = bqn[t % 4]
                    ts(TRef(qn.ap[:, 0:64], qn.buf), TRef(po.ap[:, 0:64], p.buf),
                       r0, ALU.mult)
                    ts(TRef(qn.ap[:, 64:128], qn.buf), TRef(po.ap[:, 64:128], p.buf),
                       r1, ALU.mult)
                    if t >= 1:
                        bfinish(t - 1)
                    if t == NT_Q - 1:
                        bfinish(t)
                else:
                    vs = bvs[t % 4]
                    copy(TRef(vs.ap[:, 0:64], vs.buf), TRef(po.ap[:, 0:64], p.buf))
                    copy(TRef(vs.ap[:, 65:129], vs.buf),
                         TRef(po.ap[:, 64:128], p.buf), eng="scalar")
                    dma(dref(outdram[pr, t], oname), vs, eng="gpsimd")

    P.barrier()
    ar.reset(ov)
    ar16.reset(ov16)
    wob16 = ar16.alloc("wob16", DC * 2048)
    aoq16 = [ar16.alloc(f"aoq16{c}", 1024) for c in range(NPAIR)]
    bqslh = [ar16.alloc(f"bqh{i}", 512) for i in range(2)]
    bqsll = [ar16.alloc(f"bql{i}", 512) for i in range(2)]
    bkslh = [ar16.alloc(f"bkh{i}", 640) for i in range(2)]
    bksll = [ar16.alloc(f"bkl{i}", 640) for i in range(2)]
    bvsl = [ar.alloc(f"bv{i}", 5 * 130) for i in range(2)]
    bms = ar.alloc("bms", 5 * 512)
    aoq = [ar.alloc(f"aoq{c}", 512) for c in range(NPAIR)]
    besc = [[ar.alloc(f"besc{h}{i}", 512) for i in range(3)] for h in range(2)]
    bes = [[ar.alloc(f"bes{h}{i}", 512) for i in range(3)] for h in range(2)]
    brd = [ar.alloc(f"brd{h}", 512) for h in range(2)]
    bxr = [ar.alloc(f"bxr{i}", 1024) for i in range(2)]
    bx4 = [ar.alloc(f"bx4{i}", 1024) for i in range(2)]

    for c in range(DC):
        dma(TRef(wob16.ap[:, c * 2048:(c + 1) * 2048].rearrange(
            "p (hl j) -> p hl j", hl=2), wob16.buf),
            TRef(wobsp_d[c].rearrange("hl p j -> p hl j"), dbuf["wsp"]))

    libi_of = {}
    for li, (qi, k) in enumerate(LIVE_B):
        libi_of[(qi, k)] = li

    for qi, (qs, qw) in enumerate(QCS):
        kts = [k for (qj, k) in LIVE_B if qj == qi]
        k0 = kts[0]
        nk = len(kts)
        for ki, k in enumerate(kts):
            li = libi_of[(qi, k)]
            dma(TRef(bms.ap[:, ki * 512:ki * 512 + qw], bms.buf),
                TRef(bmask_d[li][:, 0:qw], Buf("bmd")))
        for pr in range(NPAIR):
            qslh = bqslh[pr % 2]
            qsll = bqsll[pr % 2]
            kslh = bkslh[pr % 2]
            ksll = bksll[pr % 2]
            vsl = bvsl[pr % 2]
            dma(TRef(qslh.ap[:, 0:qw], qslh.buf),
                dref(qbh_d[pr][:, qs:qs + qw], "qbh"))
            dma(TRef(qsll.ap[:, 0:qw], qsll.buf),
                dref(qbl_d[pr][:, qs:qs + qw], "qbl"))
            dma(TRef(kslh.ap[:, 0:nk * 128], kslh.buf),
                dref(kbh_d[pr][:, k0 * 128:(k0 + nk) * 128], "kbh"))
            dma(TRef(ksll.ap[:, 0:nk * 128], ksll.buf),
                dref(kbl_d[pr][:, k0 * 128:(k0 + nk) * 128], "kbl"))
            for ki in range(nk):
                dma(TRef(vsl.ap[:, ki * 130:(ki + 1) * 130], vsl.buf),
                    dref(vb_d[pr, k0 + ki], "vb"))
            pvp = [TRef(pb[6].ap[0:65, 0:qw], pb[6].buf),
                   TRef(pb[7].ap[0:65, 0:qw], pb[7].buf)]

            def emit_bpv(ki, _pvp=pvp, _nk=nk, _qw=qw, _vsl=vsl):
                for h in range(2):
                    e_t = TRef(bes[h][ki % 3].ap[:, 0:_qw], bes[h][ki % 3].buf)
                    mm(_pvp[h], TRef(_vsl.ap[:, ki * 130 + h * 65:
                                             ki * 130 + (h + 1) * 65], _vsl.buf),
                       e_t, start=(ki == 0), stop=(ki == _nk - 1), tag="bpv")

            pend = []
            for ki, k in enumerate(kts):
                bm = TRef(bms.ap[:, ki * 512:ki * 512 + qw], bms.buf)
                pos = [TRef(pb[3 * h + ki % 3].ap[:, 0:qw],
                            pb[3 * h + ki % 3].buf) for h in range(2)]
                qk_ops = [(kslh, qslh), (kslh, qsll), (ksll, qslh)]
                for pi, (ka, qa) in enumerate(qk_ops):
                    for h in range(2):
                        mm(pos[h],
                           TRef(ka.ap[h * 64:(h + 1) * 64,
                                      ki * 128:(ki + 1) * 128], ka.buf),
                           TRef(qa.ap[h * 64:(h + 1) * 64, 0:qw], qa.buf),
                           start=(pi == 0), stop=(pi == 2), tag="bqk")
                if len(pend) >= 2:
                    emit_bpv(pend.pop(0))
                for h in range(2):
                    hh = pr * 2 + h
                    sc = TRef(besc[h][ki % 3].ap[:, 0:qw], besc[h][ki % 3].buf)
                    stt(sc, bm, TRef(slope.ap[:, hh:hh + 1], slope.buf), pos[h],
                        ALU.mult, ALU.add, tag="bbias")
                    e_t = TRef(bes[h][ki % 3].ap[:, 0:qw], bes[h][ki % 3].buf)
                    act(e_t, sc, AF.Exp, tag="bexp")
                pend.append(ki)
            for p_ in pend:
                emit_bpv(p_)
            for h in range(2):
                rds = TRef(brd[h].ap[0:1, 0:qw], brd[h].buf)
                recip(rds, TRef(pvp[h].ap[64:65, :], pvp[h].buf))
                pbc = pb[h]
                bc = TRef(pbc.ap[0:64, 0:qw], pbc.buf)
                mm(bc, TRef(ones.ap[0:1, 0:64], ones.buf), rds,
                   start=True, stop=True, tag="bbc")
                bcs = TRef(bes[h][0].ap[0:64, 0:qw], bes[h][0].buf)
                copy(bcs, bc, eng="scalar")
                tt(TRef(aoq[pr].ap[h * 64:(h + 1) * 64, 0:qw], aoq[pr].buf),
                   TRef(pvp[h].ap[0:64, :], pvp[h].buf), bcs, ALU.mult)
            split(TRef(aoq16[pr].ap[:, 0:qw], aoq16[pr].buf),
                  TRef(aoq16[pr].ap[:, 512:512 + qw], aoq16[pr].buf),
                  TRef(aoq[pr].ap[:, 0:qw], aoq[pr].buf))
        for ti in range(qw // 128):
            t = qs // 128 + ti
            xr = bxr[ti % 2]
            dma(xr, dref(x3_d[t], "x3"))
            for dh in range(2):
                p = pb[2 * (ti % 2) + dh]
                po = TRef(p.ap, p.buf)
                for c in range(NPAIR):
                    mm3(po,
                        (TRef(aoq16[c].ap[:, ti * 128:(ti + 1) * 128],
                              aoq16[c].buf),
                         TRef(aoq16[c].ap[:, 512 + ti * 128:512 + (ti + 1) * 128],
                              aoq16[c].buf)),
                        (TRef(wob16.ap[:, c * 2048 + dh * 512:
                                       c * 2048 + (dh + 1) * 512], wob16.buf),
                         TRef(wob16.ap[:, c * 2048 + 1024 + dh * 512:
                                       c * 2048 + 1024 + (dh + 1) * 512],
                              wob16.buf)),
                        start=(c == 0), stop=(c == NPAIR - 1), tag="boproj")
                xs = bx4[ti % 2]
                tt(TRef(xs.ap[:, dh * 512:(dh + 1) * 512], xs.buf), po,
                   TRef(xr.ap[:, dh * 512:(dh + 1) * 512], xr.buf), ALU.add)
            dma(dref(x4_d[t], "x4"), bx4[ti % 2], eng="gpsimd")

    yout_tiles = [yout_d[t * 128:(t + 1) * 128, :] for t in range(NT_Q)]
    emit_ffn_sparse(x4_d, "x4", yout_tiles, "yout")

    P.materialize(stack)
    stack.close()
    return nc


_NC_CACHE = {}


def _get_nc():
    if "nc" not in _NC_CACHE:
        _NC_CACHE["nc"] = build_nc()
    return _NC_CACHE["nc"]


def _pack_proj_w(Wdm):
    return np.ascontiguousarray(
        Wdm.reshape(8, 128, 8, 128).transpose(2, 1, 0, 3).reshape(8, 128, 1024))


def _build_in_maps(L0_attn_a_x, L0_attn_a_wq, L0_attn_a_wk, L0_attn_a_wv,
                   L0_attn_a_wo, L0_attn_b_Wq, L0_attn_b_Wk, L0_attn_b_Wv,
                   L0_attn_b_Wo, L0_ffn_rw1, L0_ffn_rb1, L0_ffn_rw2, L0_ffn_rb2,
                   L0_ffn_ew1, L0_ffn_eb1, L0_ffn_ew2, L0_ffn_eb2):
    f32 = np.float32
    x = np.asarray(L0_attn_a_x, f32)
    wq = np.asarray(L0_attn_a_wq, f32)
    wk = np.asarray(L0_attn_a_wk, f32)
    wv = np.asarray(L0_attn_a_wv, f32)
    wo = np.asarray(L0_attn_a_wo, f32)
    Wbq = np.asarray(L0_attn_b_Wq, f32)
    Wbk = np.asarray(L0_attn_b_Wk, f32)
    Wbv = np.asarray(L0_attn_b_Wv, f32)
    Wbo = np.asarray(L0_attn_b_Wo, f32)
    rw1 = np.asarray(L0_ffn_rw1, f32)
    rb1 = np.asarray(L0_ffn_rb1, f32)
    rw2 = np.asarray(L0_ffn_rw2, f32)
    rb2 = np.asarray(L0_ffn_rb2, f32)
    ew1 = np.asarray(L0_ffn_ew1, f32)
    eb1 = np.asarray(L0_ffn_eb1, f32)
    ew2 = np.asarray(L0_ffn_ew2, f32)
    eb2 = np.asarray(L0_ffn_eb2, f32)

    wq_dm = wq.transpose(2, 0, 1).reshape(D, D)
    wk_dm = wk.transpose(2, 0, 1).reshape(D, D)
    wv_dm = wv.transpose(2, 0, 1).reshape(D, D)
    wq_dev = _pack_proj_w(wq_dm)
    wk_dev = _pack_proj_w(wk_dm)
    wv_dev = _pack_proj_w(wv_dm)
    woT = np.ascontiguousarray(wo.T)
    wo_dev = np.ascontiguousarray(woT.reshape(8, 128, 1024))
    bwq_dev = _pack_proj_w(np.ascontiguousarray(Wbq.T))
    bwk_dev = _pack_proj_w(np.ascontiguousarray(Wbk.T))
    bwv_dev = _pack_proj_w(np.ascontiguousarray(Wbv.T))
    bwo_dev = np.ascontiguousarray(Wbo.T.reshape(8, 128, 1024))
    rw1_dev = _pack_proj_w(rw1)
    rw2_dev = np.ascontiguousarray(rw2.reshape(8, 128, 8).transpose(1, 0, 2)
                                   .reshape(128, 64))
    rb1_dev = np.ascontiguousarray(rb1.reshape(8, 128).T)
    rb2_dev = np.ascontiguousarray(rb2.reshape(1, 8))
    eb1_dev = np.ascontiguousarray(eb1.reshape(8, 16, 128).transpose(2, 0, 1)
                                   .reshape(128, 128))
    eb1g_dev = np.ascontiguousarray(eb1.reshape(8, 16, 128).transpose(0, 2, 1))
    eb2_dev = np.ascontiguousarray(eb2)
    ew1_dev = np.ascontiguousarray(
        ew1.reshape(E, 8, 128, 16, 128).transpose(0, 2, 3, 1, 4)
        .reshape(E, 128, 16384))
    ew2_dev = np.ascontiguousarray(ew2.reshape(E, NHC, 128, 1024))
    hv = np.repeat(np.arange(16, dtype=f32)[None, :], 128, axis=0)

    shared = dict(wq=wq_dev, wk=wk_dev, wv=wv_dev, wo=wo_dev,
                  bwq=bwq_dev, bwk=bwk_dev, bwv=bwv_dev, bwo=bwo_dev,
                  rw1=rw1_dev, rw2=rw2_dev, rb1=rb1_dev, rb2=rb2_dev,
                  eb1=eb1_dev, eb1g=eb1g_dev, eb2=eb2_dev, ew1=ew1_dev,
                  ew2=ew2_dev)

    in_maps = []
    for core in range(8):
        b = core // 2
        parity = core % 2
        if parity == 0:
            xkv = np.concatenate([np.zeros((2048, D), f32), x[b, 0:2176]], 0)
        else:
            xkv = np.concatenate([x[b], np.zeros((128, D), f32)], 0)
        dl = attn_a_deltas(parity)
        deltas = np.ascontiguousarray(np.repeat(dl[None, :], 128, axis=0))
        bm = attn_b_bmask(parity)
        m = dict(shared)
        m["xkv"] = np.ascontiguousarray(xkv)
        m["deltas"] = deltas
        m["bmask"] = bm
        in_maps.append(m)
    return in_maps


def kernel(**inputs):
    in_maps = _build_in_maps(**inputs)
    nc = _get_nc()
    res = run_bass_kernel_spmd(nc, in_maps, core_ids=list(range(8)))
    out = np.zeros((B, S, D), np.float32)
    for core in range(8):
        b = core // 2
        parity = core % 2
        y = np.asarray(res.results[core]["y_out"])
        out[b, parity * 2048:(parity + 1) * 2048] = y[0:2048]
    return out
